# version 10
# baseline (speedup 1.0000x reference)
import os
import numpy as np
import concourse.bass as bass
import concourse.mybir as mybir
from concourse.bass_utils import run_bass_kernel_spmd

F32 = mybir.dt.float32
BF16 = mybir.dt.bfloat16
I32 = mybir.dt.int32
AF = mybir.ActivationFunctionType
ALU = mybir.AluOpType
AX = mybir.AxisListType

PI = float(np.pi)
TWO_PI = float(2 * np.pi)
LN1E4_32 = float(np.log(10000.0) / 32.0)

T = 1024
NT = 8
D = 2048
ND = 16
LAT = 512
NL = 4
NH_LOC = 8
S2 = 2048
IM = 1024
NI = 8
NE = 8
EPS = 1e-5
ISQ = float(1.0 / np.sqrt(128.0))
BSHIFT = 40.0


class _Eng:

    def __init__(self, name, auto):
        self.name = name
        self.auto = auto
        self.n = 0
        self.thunks = []
        self._sat = {}

    def wait(self, other, val=None):
        if isinstance(other, _Eng):
            semname, v = other.name, (other.n if val is None else val)
        else:
            semname, v = other.name, 16 * (other.n if val is None else val)
        if v <= 0 or v <= self._sat.get(semname, -1):
            return
        self._sat[semname] = v

        def th(e, env):
            e.wait_ge(env[semname], v)
        self.thunks.append(th)

    def op(self, fname, *args, **kwargs):
        if self.auto:
            self.wait(self, self.n)
        myname, auto = self.name, self.auto

        def th(e, env):
            inst = getattr(e, fname)(*args, **kwargs)
            if auto:
                inst.then_inc(env[myname], 1)
        self.thunks.append(th)
        if self.auto:
            self.n += 1
        return self.n

    def op_mark(self, fname, *args, **kwargs):
        myname = self.name

        def th(e, env):
            getattr(e, fname)(*args, **kwargs).then_inc(env[myname], 1)
        self.thunks.append(th)
        self.n += 1
        return self.n

    def op_raw(self, fname, *args, **kwargs):
        cell = [False]
        myname = self.name

        def th(e, env):
            inst = getattr(e, fname)(*args, **kwargs)
            if cell[0]:
                inst.then_inc(env[myname], 1)
        self.thunks.append(th)
        self._last_raw = cell

    def mark(self):
        self._last_raw[0] = True
        self._last_raw = None
        self.n += 1
        return self.n

    def raw(self, fn):
        self.thunks.append(fn)


class _Dma:
    def __init__(self, eng, name):
        self.eng = eng
        self.name = name
        self.n = 0

    def wait(self, other, val=None):
        self.eng.wait(other, val)

    def begin(self):
        self.eng.wait(self, self.n)

    def start(self, out, in_, **kw):
        myname = self.name

        def th(e, env):
            e.dma_start(out=out, in_=in_, **kw).then_inc(env[myname], 16)
        self.eng.thunks.append(th)
        self.n += 1
        return self.n

    def start_dyn(self, outf, inf, tag="", **kw):
        myname = self.name

        def th(e, env):
            try:
                e.dma_start(out=outf(env), in_=inf(env), **kw).then_inc(
                    env[myname], 16)
            except Exception:
                raise RuntimeError(f"start_dyn failed tag={tag}")
        self.eng.thunks.append(th)
        self.n += 1
        return self.n


class _Slot:

    def __init__(self, view):
        self.view = view
        self.cons = []

    def acquire(self, te):
        for eng, cnt in self.cons:
            te.wait(eng, cnt)
        self.cons = []

    def consumed(self, eng):
        self.cons.append((eng, eng.n))


def ts(i, sz):
    return slice(i * sz, (i + 1) * sz)


KPHASE = int(os.environ.get("KPHASE", "9"))
KGRU = int(os.environ.get("KGRU", "9"))


def build_nc():
    nc = bass.Bass("TRN2", target_bir_lowering=False)
    dt = nc.dram_tensor

    x_in = dt("x", [T, D], F32, kind="ExternalInput").ap()
    pos_in = dt("pos", [1, T], F32, kind="ExternalInput").ap()
    h0_in = dt("h0", [1, 1], F32, kind="ExternalInput").ap()
    cmask_in = dt("cmask", [1, 1], F32, kind="ExternalInput").ap()
    meta_in = dt("meta", [1, 8], I32, kind="ExternalInput").ap()

    wzt = dt("wzt", [ND, D, 128], F32, kind="ExternalInput").ap()
    wnt = dt("wnt", [ND, D, 128], F32, kind="ExternalInput").ap()
    bz_r = dt("bz_r", [128, ND], F32, kind="ExternalInput").ap()
    bn_r = dt("bn_r", [128, ND], F32, kind="ExternalInput").ap()
    ln1_r = dt("ln1_r", [128, ND], F32, kind="ExternalInput").ap()
    ln2_row = dt("ln2_row", [1, D], F32, kind="ExternalInput").ap()
    kvd_t = dt("kvd_t", [NL, D, 128], F32, kind="ExternalInput").ap()
    qd_t = dt("qd_t", [NL, D, 128], F32, kind="ExternalInput").ap()
    ku_w = dt("ku_w", [LAT, 1024], F32, kind="ExternalInput").ap()
    qu_w = dt("qu_w", [LAT, 1024], F32, kind="ExternalInput").ap()
    vu_w = dt("vu_w", [LAT, D], F32, kind="ExternalInput").ap()
    rpk_t = dt("rpk_t", [8, D, 128], F32, kind="ExternalInput").ap()
    rpks_t = dt("rpks_t", [8, D, 128], F32, kind="ExternalInput").ap()
    rpq_t = dt("rpq_t", [8, LAT, 128], F32, kind="ExternalInput").ap()
    rpqs_t = dt("rpqs_t", [8, LAT, 128], F32, kind="ExternalInput").ap()
    ow_t = dt("ow_t", [4, D, 512], F32, kind="ExternalInput").ap()
    routw = dt("routw", [D, 7], F32, kind="ExternalInput").ap()
    routb = dt("routb", [1, 7], F32, kind="ExternalInput").ap()
    eg_t = dt("eg_t", [NE, 4, D, 256], F32, kind="ExternalInput").ap()
    eu_t = dt("eu_t", [NE, 4, D, 256], F32, kind="ExternalInput").ap()
    ed_t = dt("ed_t", [NE, 4, IM, 512], F32, kind="ExternalInput").ap()

    out_hbm = dt("out", [T, D], F32, kind="ExternalOutput").ap()

    cc_qk_in = dt("cc_qk_in", [4096, T], BF16).ap()
    cc_qk_out = dt("cc_qk_out", [2 * 4096, T], BF16).ap()
    cc_vpush = dt("cc_vpush", [T, D], BF16).ap()
    cc_vgath = dt("cc_vgath", [2 * T, D], BF16).ap()
    cc_cr_in = dt("cc_cr_in", [1, D], F32).ap()
    cc_cr_out = dt("cc_cr_out", [2, D], F32).ap()
    cc_o_in = dt("cc_o_in", [1024, S2], BF16).ap()
    cc_o_out = dt("cc_o_out", [2 * 1024, S2], BF16).ap()
    myqk = dt("myqk", [4096, T], BF16).ap()
    myv = dt("myv", [2048, T], BF16).ap()
    myo = dt("myo", [2048, T], BF16).ap()
    moe_bf = dt("moe_bf", [128, 425984], BF16).ap()
    sc_h = dt("sc_h", [D, T], F32).ap()
    sc_p = dt("sc_p", [D, T], F32).ap()
    sc_at = dt("sc_at", [T, D], F32).ap()

    from contextlib import ExitStack
    ctx = ExitStack()
    sbuf = lambda n, s, d: ctx.enter_context(nc.sbuf_tensor(n, s, d))
    psum = lambda n, s, d: ctx.enter_context(nc.psum_tensor(n, s, d))
    mksem = lambda n: ctx.enter_context(nc.semaphore(n))

    with ctx:
        xh = sbuf("xh", [128, NT * D], F32)
        big1 = sbuf("big1", [128, ND * T], BF16)
        big2 = sbuf("big2", [128, ND * T], BF16)
        wring = sbuf("wring", [128, 12288], BF16)
        pool3 = sbuf("pool3", [128, 5120], F32)
        hE = sbuf("hE", [128, 6144], BF16)
        stg = sbuf("stg", [128, 2048], BF16)
        stats = sbuf("stats", [128, 192], F32)
        cst_bf = sbuf("cst_bf", [128, 256], BF16)
        cst_f = sbuf("cst_f", [128, 128], F32)
        onescol_b = sbuf("onescol_b", [128, 1], BF16)
        onesrow_b = sbuf("onesrow_b", [1, 128], BF16)
        onesrow_f = sbuf("onesrow_f", [1, 128], F32)
        rowpool = sbuf("rowpool", [1, 3840], F32)
        meta_sb = sbuf("meta_sb", [1, 8], I32)
        h0col = sbuf("h0col", [1, 1], F32)
        cmk = sbuf("cmk", [1, 1], F32)
        routb_sb = sbuf("routb_sb", [1, 7], F32)
        posrow = rowpool[0:1, 2816:3840]
        rs1row = rowpool[0:1, 0:1024]
        crrow = rowpool[0:1, 0:2048]
        q2row = rowpool[0:1, 0:2048]
        ln2row_sb = rowpool[0:1, 0:2048]
        k2row = rowpool[0:1, 2048:2560]
        denrow = rowpool[0:1, 2048:2560]
        mxcell = rowpool[0:1, 2560:2561]
        mx1 = rowpool[0:1, 2561:2562]
        mx2 = rowpool[0:1, 2562:2563]
        iotarow = rowpool[0:1, 2048:2304]
        nbrow = rowpool[0:1, 2816:3840].bitcast(BF16)

        ps6 = [psum(f"ps{i}", [128, 512], F32) for i in range(6)]
        s0, s1, s2, s3, s4, s5 = ps6
        p3 = psum("p3", [128, 2048], BF16)

        semnames = (["te", "ve", "sc", "gp", "qs", "qg", "cc", "so"]
                    + [f"qw{i}" for i in range(8)] + ["qh0", "qh1"]
                    + ["sp0", "sp1", "rl0", "rl1", "at0", "at1",
                       "pp0", "pp1", "pp2", "pp3", "crq", "cpq", "ocq", "orq",
                       "qd0", "qd1", "qd2", "qd3"])
        sems = {n: mksem("s_" + n) for n in semnames}

        te = _Eng("te", auto=False)
        ve = _Eng("ve", auto=True)
        sc = _Eng("sc", auto=True)
        gp = _Eng("gp", auto=True)
        sy = _Eng("sy", auto=False)
        qs = _Dma(sy, "qs")
        qg = _Dma(gp, "qg")
        qo = _Dma(sy, "so")
        qws = [_Dma(gp, f"qw{i}") for i in range(8)]
        qds = [_Dma(gp, f"qd{i}") for i in range(4)]
        qw_i = [0]
        qhs = [_Dma(sy, "qh0"), _Dma(sy, "qh1")]
        qsp = [_Dma(sy, "sp0"), _Dma(sy, "sp1")]
        qrl = [_Dma(sy, "rl0"), _Dma(sy, "rl1")]
        qat = [_Dma(sy, "at0"), _Dma(sy, "at1")]
        qpp = [_Dma(sy, f"pp{i}") for i in range(4)]
        qpp_i = [0]
        pp_done = [None, None]
        crq = _Dma(sy, "crq")
        cpq = _Dma(sy, "cpq")
        ocq = _Dma(sy, "ocq")
        orq = _Dma(sy, "orq")

        p0A, p0B = _Slot(s0[:, :]), _Slot(s1[:, :])
        p1A, p1B = _Slot(s2[:, :]), _Slot(s3[:, :])
        p2A, p2B = _Slot(s4[:, :]), _Slot(s5[:, :])
        p3A, p3B = _Slot(p3[:, 0:128]), _Slot(p3[:, 1024:1152])

        PC = ts(0, 1024)
        PV = ts(1, 1024)
        PH = ts(2, 1024)
        PP = ts(3, 1024)
        PW = ts(4, 1024)
        PSQ = slice(0, 2048)
        PBC = ts(2, 1024)
        PLN2 = slice(3072, 5120)
        PXF = slice(0, 2048)
        PAF = slice(2048, 4096)
        zeros_f = hE[:, 4096:6144].bitcast(F32)

        C_NEGM, C_RSTD, C_RS1, C_RS2 = 0, 8, 16, 24
        C_BZ, C_NBZ, C_BN, C_BNP, C_LN1 = 32, 48, 64, 80, 96
        C_WRT = 112
        C_ONE, C_INVF, C_SGN, C_CAR = 168, 169, 170, 171
        C_EPS, C_E12 = 184, 185
        C_TMP = 172


        def _emit():
            env = dict(sems)
            env["sy"] = sems["qs"]
            blk = ctx.enter_context(nc.Block())

            @blk.sync
            def _(e):
                for th in sy.thunks:
                    th(e, env)

            @blk.gpsimd
            def _(e):
                for th in gp.thunks:
                    th(e, env)

            @blk.vector
            def _(e):
                for th in ve.thunks:
                    th(e, env)

            @blk.scalar
            def _(e):
                for th in sc.thunks:
                    th(e, env)

            @blk.tensor
            def _(e):
                for th in te.thunks:
                    th(e, env)

        def _early_finish():
            for i_ in range(NT):
                qo.wait(ve, ve.n)
                qo.wait(sc, sc.n)
                qo.wait(te, te.n)
                qo.wait(gp, gp.n)
                qo.start(out_hbm[ts(i_, 128), :], xh[:, ts(i_, D)])
            qof = 16 * qo.n

            def fin(e, env):
                e.wait_ge(env["so"], qof)
            sy.raw(fin)
            _emit()
            return nc

        for i in range(NT):
            qs.start(xh[:, ts(i, D)], x_in[ts(i, 128), :])
        qs.start(posrow, pos_in[:, :])
        qs.start(meta_sb[:, :], meta_in[:, :])
        qs.start(h0col[:, :], h0_in[:, :])
        qs.start(cmk[:, :], cmask_in[:, :])
        qs.start(stats[:, C_BZ:C_BZ + 16], bz_r[:, :])
        qs.start(stats[:, C_BN:C_BN + 16], bn_r[:, :])
        qs.start(stats[:, C_LN1:C_LN1 + 16], ln1_r[:, :])
        qs.start(routb_sb[:, :], routb[:, :])
        N_LD0 = qs.n

        dynv = {}

        def regload_thunk(e, env):
            for idx, nm, mx in [(0, "m_qk", 1024), (1, "m_vr", 0),
                                (2, "m_vc", 1024), (3, "m_or", 0),
                                (4, "m_oc", 1024), (5, "m_cr", 0)]:
                r = e.alloc_register(nm)
                e.reg_load(r, meta_sb[0:1, idx:idx + 1])
                dynv[nm] = e.snap(r, min_val=0, max_val=mx)
        sy.wait(qs, N_LD0)
        sy.raw(regload_thunk)

        gp.op("memset", cst_bf[:, 0:128], 0.0)
        gp.op("affine_select", out=cst_bf[:, 0:128], in_=cst_bf[:, 0:128],
              compare_op=ALU.not_equal, fill=1.0, base=0,
              pattern=[[-1, 128]], channel_multiplier=1)
        gp.op("memset", cst_bf[:, 128:256], 1.0)
        gp.op("affine_select", out=cst_bf[:, 128:256], in_=cst_bf[:, 128:256],
              compare_op=ALU.is_ge, fill=0.0, base=0,
              pattern=[[1, 128]], channel_multiplier=-1)
        gp.op("memset", cst_f[:, :], 0.0)
        gp.op("affine_select", out=cst_f[:, :], in_=cst_f[:, :],
              compare_op=ALU.not_equal, fill=1.0, base=0,
              pattern=[[-1, 128]], channel_multiplier=1)
        gp.op("memset", onescol_b[:, :], 1.0)
        gp.op("memset", onesrow_b[:, :], 1.0)
        gp.op("memset", onesrow_f[:, :], 1.0)
        gp.op("memset", hE[:, 4096:6144], 0.0)
        gp.op("memset", stats[:, C_ONE:C_ONE + 1], 1.0)
        gp.op("memset", stats[:, C_EPS:C_EPS + 1], EPS)
        gp.op("memset", stats[:, C_E12:C_E12 + 1], 1e-12)
        gp.op("iota", iotarow[0:1, 0:128], pattern=[[0, 4], [1, 32]],
              base=0, channel_multiplier=0,
              allow_small_or_imprecise_dtypes=True)
        gp.op("iota", iotarow[0:1, 128:256], pattern=[[0, 2], [1, 2], [0, 32]],
              base=0, channel_multiplier=0,
              allow_small_or_imprecise_dtypes=True)
        GP_CONST = gp.n

        ident_b = cst_bf[:, 0:128]
        causal = cst_bf[:, 128:256]
        ident_f = cst_f[:, :]

        ve.wait(qs, N_LD0)
        ve.op("tensor_scalar_mul", stats[:, C_NBZ:C_NBZ + 16],
              stats[:, C_BZ:C_BZ + 16], -1.0)
        ve.op("tensor_scalar_add", stats[:, C_BNP:C_BNP + 16],
              stats[:, C_BN:C_BN + 16], 0.5)

        te.wait(gp, GP_CONST)
        p2A.acquire(te)
        te.op_mark("matmul", s4[:, 0:1], iotarow[0:1, 0:128], onesrow_f[0:1, 0:1],
                   start=True, stop=True)
        te.op_mark("matmul", s4[:, 1:2], iotarow[0:1, 128:256], onesrow_f[0:1, 0:1],
                   start=True, stop=True)
        TE_IOTA = te.n
        sc.wait(te, TE_IOTA)
        sc.op("activation", stats[:, C_INVF:C_INVF + 1], s4[:, 0:1], AF.Exp,
              scale=-LN1E4_32)
        ve.wait(te, TE_IOTA)
        ve.wait(sc, sc.n)
        ve.op("tensor_scalar", stats[:, C_SGN:C_SGN + 1], s4[:, 1:2],
              2.0, -1.0, ALU.mult, ALU.add)
        p2A.consumed(ve)
        p2A.consumed(sc)

        for i in range(NT):
            xt = xh[:, ts(i, D)]
            ve.op("tensor_reduce", stats[:, C_TMP:C_TMP + 1], xt, AX.X, ALU.add)
            ve.op("tensor_tensor", pool3[:, PSQ], xt, xt, ALU.mult)
            ve.op("tensor_reduce", stats[:, C_TMP + 1:C_TMP + 2], pool3[:, PSQ],
                  AX.X, ALU.add)
            ve.op("tensor_scalar_mul", stats[:, C_NEGM + i:C_NEGM + i + 1],
                  stats[:, C_TMP:C_TMP + 1], -1.0 / D)
            ve.op("tensor_scalar_mul", stats[:, C_TMP + 1:C_TMP + 2],
                  stats[:, C_TMP + 1:C_TMP + 2], 1.0 / D)
            ve.op("tensor_tensor", stats[:, C_TMP + 2:C_TMP + 3],
                  stats[:, C_NEGM + i:C_NEGM + i + 1],
                  stats[:, C_NEGM + i:C_NEGM + i + 1], ALU.mult)
            ve.op("tensor_tensor", stats[:, C_TMP + 1:C_TMP + 2],
                  stats[:, C_TMP + 1:C_TMP + 2],
                  stats[:, C_TMP + 2:C_TMP + 3], ALU.subtract)
            sc.wait(ve, ve.n)
            sc.op("activation", stats[:, C_TMP + 3:C_TMP + 4],
                  stats[:, C_TMP + 1:C_TMP + 2], AF.Ln, bias=stats[:, C_EPS:C_EPS + 1])
            sc.op("activation", stats[:, C_RSTD + i:C_RSTD + i + 1],
                  stats[:, C_TMP + 3:C_TMP + 4], AF.Exp, scale=-0.5)
            ve.wait(sc, sc.n)
            ve.op("tensor_tensor", stats[:, C_TMP + 4:C_TMP + 5],
                  stats[:, C_TMP + 1:C_TMP + 2],
                  stats[:, C_RSTD + i:C_RSTD + i + 1], ALU.mult)
            ve.op("tensor_tensor", stats[:, C_TMP + 4:C_TMP + 5],
                  stats[:, C_TMP + 4:C_TMP + 5],
                  stats[:, C_RSTD + i:C_RSTD + i + 1], ALU.mult)
            sc.wait(ve, ve.n)
            sc.op("activation", stats[:, C_TMP + 5:C_TMP + 6],
                  stats[:, C_TMP + 4:C_TMP + 5], AF.Ln, bias=stats[:, C_EPS:C_EPS + 1])
            sc.op("activation", stats[:, C_RS1 + i:C_RS1 + i + 1],
                  stats[:, C_TMP + 5:C_TMP + 6], AF.Exp, scale=-0.5)
            ve.wait(sc, sc.n)
            ve.op("tensor_scalar", xt, xt,
                  stats[:, C_NEGM + i:C_NEGM + i + 1],
                  stats[:, C_RSTD + i:C_RSTD + i + 1], ALU.add, ALU.mult)
            ve.op("tensor_copy", big1[:, ts(i, D)], xt)
        VE_LN = ve.n
        SC_LN = sc.n

        te.wait(ve, VE_LN)
        p2A.acquire(te)
        p2B.acquire(te)
        for i in range(NT):
            tgt = s4 if i < 4 else s5
            te.op_raw("matmul", tgt[0:1, ts(i % 4, 128)],
                      stats[:, C_RS1 + i:C_RS1 + i + 1], ident_f,
                      start=True, stop=True)
        te.mark()
        TE_RS1ROW = te.n
        ve.wait(te, TE_RS1ROW)
        ve.op("tensor_copy", rs1row[0:1, 0:512], s4[0:1, :])
        ve.op("tensor_copy", rs1row[0:1, 512:1024], s5[0:1, :])
        p2A.consumed(ve)
        p2B.consumed(ve)
        VE_RS1ROW = ve.n
        te.wait(ve, VE_RS1ROW)
        p2A.acquire(te)
        p2B.acquire(te)
        te.op_raw("matmul", s4[:, :], onesrow_f[0:1, :], rs1row[0:1, 0:512],
                  start=True, stop=True)
        te.op_mark("matmul", s5[:, :], onesrow_f[0:1, :],
                   rs1row[0:1, 512:1024], start=True, stop=True)
        TE_RS1B = te.n
        ve.wait(te, TE_RS1B)
        ve.op("tensor_copy", pool3[:, PBC][:, 0:512], s4[:, :])
        ve.op("tensor_copy", pool3[:, PBC][:, 512:1024], s5[:, :])
        p2A.consumed(ve)
        p2B.consumed(ve)
        VE_RS1B = ve.n

        tslots = [p3A, p3B]
        tp = 0
        te.wait(ve, VE_LN)
        for ec in range(ND):
            for i in range(NT):
                sl = tslots[tp % 2]
                sl.acquire(te)
                te.op_mark("transpose", sl.view,
                           big1[:, slice(i * D + ec * 128, i * D + (ec + 1) * 128)],
                           ident_b)
                ve.wait(te, te.n)
                ve.op("tensor_copy",
                      big2[:, slice(ec * T + i * 128, ec * T + (i + 1) * 128)],
                      sl.view)
                sl.consumed(ve)
                tp += 1
        VE_XNFM = ve.n

        for ec in range(ND):
            ve.op("scalar_tensor_tensor", big1[:, ts(ec, T)], big2[:, ts(ec, T)],
                  stats[:, C_LN1 + ec:C_LN1 + ec + 1], pool3[:, PBC],
                  ALU.mult, ALU.mult)
        VE_RMS1 = ve.n

        ring = {"off": 0, "segs": []}

        RINGW = 12288

        def ring_align(a):
            off = ring["off"]
            pad = (-off) % a
            if pad:
                ring["off"] = 0 if off + pad >= RINGW else off + pad

        def wload(dram_ap3, cols_bf):
            off = ring["off"]
            if off + cols_bf > RINGW:
                off = 0
                ring["off"] = 0
            keep = []
            rel = []
            for sgm in ring["segs"]:
                if sgm[0] < off + cols_bf and sgm[1] > off:
                    rel.append(sgm[2])
                else:
                    keep.append(sgm)
            ring["segs"] = keep
            for r in rel:
                if r[0]:
                    gp.wait(te, r[0])
            dst = wring[:, off:off + cols_bf]
            inner = dram_ap3.shape[2]
            q = qws[qw_i[0] % 8]
            qw_i[0] += 1
            gp.wait(q, q.n)
            q.start(dst.rearrange("p (a b) -> p a b", b=inner), dram_ap3)
            entry = (off, off + cols_bf, [0])
            ring["segs"].append(entry)
            ring["off"] = off + cols_bf
            return dst, (q, q.n), entry[2]

        if KPHASE <= 1:
            return _early_finish()

        wz_v, wz_n, wz_r = wload(wzt[0].rearrange("(dc p) e -> p dc e", p=128), D)
        wn_v, wn_n, wn_r = wload(wnt[0].rearrange("(dc p) e -> p dc e", p=128), D)
        pk = [p0A, p0B]
        pn = [p1A, p1B]
        for ec in range(ND):
            nxt = None
            if ec + 1 < ND:
                nxt = (wload(wzt[ec + 1].rearrange("(dc p) e -> p dc e", p=128), D),
                       wload(wnt[ec + 1].rearrange("(dc p) e -> p dc e", p=128), D))
            te.wait(ve, VE_XNFM)
            te.wait(wz_n[0], wz_n[1])
            te.wait(wn_n[0], wn_n[1])
            if ec == 0:
                sc.wait(ve, VE_RMS1)
            if ec > 0:
                sc.wait(qsp[(ec - 1) % 2], qsp[(ec - 1) % 2].n)
                ve.wait(qsp[(ec - 1) % 2], qsp[(ec - 1) % 2].n)
            bzc = stats[:, C_BZ + ec:C_BZ + ec + 1]
            nbzc = stats[:, C_NBZ + ec:C_NBZ + ec + 1]
            bnc = stats[:, C_BN + ec:C_BN + ec + 1]
            bnp = stats[:, C_BNP + ec:C_BNP + ec + 1]
            for th_ in range(2):
                tsl = ts(th_, 512)
                pk[th_].acquire(te)
                for dc in range(ND):
                    te.op_raw("matmul", pk[th_].view, wz_v[:, ts(dc, 128)],
                              big2[:, slice(dc * T + th_ * 512, dc * T + th_ * 512 + 512)],
                              start=(dc == 0), stop=(dc == ND - 1))
                te.mark()
                tek = te.n
                pn[th_].acquire(te)
                for dc in range(ND):
                    te.op_raw("matmul", pn[th_].view, wn_v[:, ts(dc, 128)],
                              big2[:, slice(dc * T + th_ * 512, dc * T + th_ * 512 + 512)],
                              start=(dc == 0), stop=(dc == ND - 1))
                te.mark()
                ten = te.n
                if KGRU < 2:
                    ve.wait(te, tek)
                    ve.wait(te, ten)
                    ve.op("tensor_copy", pool3[:, PC][:, tsl], pk[th_].view)
                    pk[th_].consumed(ve)
                    ve.op("tensor_copy", pool3[:, PV][:, tsl], pn[th_].view)
                    pn[th_].consumed(ve)
                    ve.op("tensor_copy", pool3[:, PH][:, tsl], pool3[:, PC][:, tsl])
                    ve.op("tensor_copy", pool3[:, PP][:, tsl], pool3[:, PV][:, tsl])
                    continue
                sc.wait(te, tek)
                sc.op("activation", pool3[:, PC][:, tsl], pk[th_].view, AF.Sigmoid,
                      scale=-1.0, bias=nbzc)
                sc.op("activation", pool3[:, PV][:, tsl], pk[th_].view, AF.Sigmoid,
                      scale=1.0, bias=bzc)
                pk[th_].consumed(sc)
                sc.wait(te, ten)
                sc.op("activation", pool3[:, PH][:, tsl], pn[th_].view, AF.Sigmoid,
                      scale=1.0, bias=bnc)
                pn[th_].consumed(sc)
                scz = sc.n
                ve.wait(te, ten)
                ve.wait(sc, scz)
                ve.op("tensor_scalar", pool3[:, PP][:, tsl], pn[th_].view,
                      bnp, None, ALU.add)
                ve.op("tensor_scalar", pool3[:, PW][:, tsl], pn[th_].view,
                      bnc, 0.0, ALU.add, ALU.is_ge)
                pn[th_].consumed(ve)
                ve.wait(sc, scz)
                ve.op("tensor_tensor", pool3[:, PP][:, tsl], pool3[:, PP][:, tsl],
                      pool3[:, PH][:, tsl], ALU.subtract)
                ve.op("tensor_tensor", pool3[:, PP][:, tsl], pool3[:, PP][:, tsl],
                      pool3[:, PW][:, tsl], ALU.mult)
                ve.op("tensor_tensor", pool3[:, PH][:, tsl], pool3[:, PH][:, tsl],
                      pool3[:, PP][:, tsl], ALU.add)
                ve.op("tensor_tensor", pool3[:, PV][:, tsl], pool3[:, PV][:, tsl],
                      pool3[:, PH][:, tsl], ALU.mult)
            if KGRU >= 3:
                ve.op("tensor_tensor_scan", out=pool3[:, PH], data0=pool3[:, PC],
                      data1=pool3[:, PV], initial=0.0,
                      op0=ALU.mult, op1=ALU.add)
                ve.op("tensor_tensor_scan", out=pool3[:, PP], data0=pool3[:, PC],
                      data1=zeros_f, initial=1.0, op0=ALU.mult, op1=ALU.add)
            ve.op("tensor_copy", stats[:, C_CAR:C_CAR + 1],
                  pool3[:, PH][:, 1023:1024])
            VEH = ve.n
            qb = qsp[ec % 2]
            qb.begin()
            qb.wait(ve, VEH)
            if KGRU >= 4:
                qb.start(sc_h[ts(ec, 128), :], pool3[:, PH])
                qb.start(sc_p[ts(ec, 128), :], pool3[:, PP])
            SPQ, SP_N = qb, qb.n
            if KGRU >= 5:
                te.wait(ve, VEH)
                p2A.acquire(te)
                te.op_mark("matmul", s4[0:1, 0:128], stats[:, C_CAR:C_CAR + 1],
                           ident_f, start=True, stop=True)
                ve.wait(te, te.n)
                ve.op("tensor_copy", crrow[0:1, ts(ec, 128)], s4[0:1, 0:128])
                p2A.consumed(ve)
            ve.wait(SPQ, SP_N)
            wz_r[0] = te.n
            wn_r[0] = te.n
            if nxt is not None:
                (wz_v, wz_n, wz_r), (wn_v, wn_n, wn_r) = nxt
        VE_GRU = ve.n
        qo.wait(ve, VE_GRU)
        qo.start(cc_cr_in[:, :], crrow[:, :])
        N_CR = qo.n

        if KPHASE <= 2:
            return _early_finish()

        KVD_OFF = 0
        QD_OFF = NL * T
        te.wait(ve, VE_RMS1)
        dslots = [p0A, p0B, p1A, p1B]
        dsl_i = 0

        def proj_to_big2(wt, dst_off):
            nonlocal dsl_i
            loads = [wload(wt[lc].rearrange("(dc p) e -> p dc e", p=128),
                           ND * 128) for lc in range(2)]
            for lc in range(NL):
                if lc + 2 < NL:
                    loads.append(wload(
                        wt[lc + 2].rearrange("(dc p) e -> p dc e", p=128),
                        ND * 128))
                w_v, w_n, w_r = loads[lc]
                te.wait(w_n[0], w_n[1])
                for th_ in range(2):
                    sl = dslots[dsl_i % 4]
                    dsl_i += 1
                    sl.acquire(te)
                    for dc in range(ND):
                        te.op_raw("matmul", sl.view,
                                  w_v[:, slice(dc * 128, (dc + 1) * 128)],
                                  big1[:, slice(dc * T + th_ * 512,
                                                dc * T + th_ * 512 + 512)],
                                  start=(dc == 0), stop=(dc == ND - 1))
                    te.mark()
                    ve.wait(te, te.n)
                    ve.op("tensor_copy",
                          big2[:, slice(dst_off + lc * T + th_ * 512,
                                        dst_off + lc * T + th_ * 512 + 512)],
                          sl.view)
                    sl.consumed(ve)
                w_r[0] = te.n

        proj_to_big2(kvd_t, KVD_OFF)
        proj_to_big2(qd_t, QD_OFF)
        VE_KVQD = ve.n

        ku_v, ku_n, ku_r = wload(ku_w.rearrange("(lc p) e -> p lc e", p=128), 4096)
        qu_v, qu_n, qu_r = wload(qu_w.rearrange("(lc p) e -> p lc e", p=128), 4096)

        def proj_updn(wv, wn_, src_off, row_of, scale_evac):
            nonlocal dsl_i
            te.wait(ve, VE_KVQD)
            te.wait(wn_[0], wn_[1])
            for oc in range(8):
                for th_ in range(2):
                    tsl = ts(th_, 512)
                    sl = dslots[dsl_i % 4]
                    dsl_i += 1
                    sl.acquire(te)
                    for lc in range(NL):
                        te.op_raw("matmul", sl.view,
                                  wv[:, slice((lc * 8 + oc) * 128,
                                              (lc * 8 + oc + 1) * 128)],
                                  big2[:, slice(src_off + lc * T + th_ * 512,
                                                src_off + lc * T + th_ * 512 + 512)],
                                  start=(lc == 0), stop=(lc == NL - 1))
                    te.mark()
                    ve.wait(te, te.n)
                    if pp_done[th_] is not None:
                        ve.wait(*pp_done[th_])
                    if scale_evac is None:
                        ve.op("tensor_copy", stg[:, tsl], sl.view)
                    else:
                        ve.op("tensor_scalar_mul", stg[:, tsl], sl.view, scale_evac)
                    sl.consumed(ve)
                    qb = qpp[qpp_i[0] % 4]
                    qpp_i[0] += 1
                    qb.begin()
                    qb.wait(ve, ve.n)
                    for hh in range(2):
                        h = 2 * oc + hh
                        qb.start(cc_qk_in[slice(row_of(h), row_of(h) + 64),
                                          slice(th_ * 512, th_ * 512 + 512)],
                                 stg[slice(hh * 64, hh * 64 + 64), tsl])
                    pp_done[th_] = (qb, qb.n)

        proj_updn(qu_v, qu_n, QD_OFF, lambda h: h * 128, ISQ)
        qu_r[0] = te.n
        proj_updn(ku_v, ku_n, KVD_OFF, lambda h: 2048 + h * 128, None)
        ku_r[0] = te.n

        if KPHASE <= 3:
            return _early_finish()

        te.wait(qs, N_LD0)
        p2A.acquire(te)
        p2B.acquire(te)
        te.op_raw("matmul", s4[:, :], onesrow_f[0:1, :], posrow[0:1, 0:512],
                  start=True, stop=True)
        te.op_mark("matmul", s5[:, :], onesrow_f[0:1, :],
                   posrow[0:1, 512:1024], start=True, stop=True)
        TE_POSB = te.n
        SIN_Kc = slice(0, 1024)
        COS_Kc = slice(1024, 2048)
        COS_Qc = slice(2048, 3072)
        SIN_Qc = slice(3072, 4096)
        ANGc = slice(4096, 5120)
        mi_view = stg[:, 0:2048].bitcast(I32)

        ve.wait(qs, qs.n)
        if pp_done[0] is not None:
            ve.wait(*pp_done[0])
        if pp_done[1] is not None:
            ve.wait(*pp_done[1])
        ve.wait(te, TE_POSB)
        ve.op("tensor_scalar_mul", pool3[:, ANGc][:, 0:512], s4[:, :],
              stats[:, C_INVF:C_INVF + 1])
        ve.op("tensor_scalar_mul", pool3[:, ANGc][:, 512:1024], s5[:, :],
              stats[:, C_INVF:C_INVF + 1])
        p2A.consumed(ve)
        p2B.consumed(ve)

        def redm(src_sl, dst_sl):
            ve.op("tensor_scalar_mul", pool3[:, dst_sl], pool3[:, src_sl],
                  1.0 / TWO_PI)
            ve.op("tensor_copy", mi_view[:, 0:1024], pool3[:, dst_sl])
            ve.op("tensor_copy", pool3[:, dst_sl], mi_view[:, 0:1024])
            ve.op("scalar_tensor_tensor", pool3[:, dst_sl], pool3[:, dst_sl],
                  -TWO_PI, pool3[:, src_sl], ALU.mult, ALU.add)
            ve.op("tensor_single_scalar", pool3[:, COS_Qc], pool3[:, dst_sl],
                  PI, ALU.is_gt)
            ve.op("scalar_tensor_tensor", pool3[:, dst_sl], pool3[:, COS_Qc],
                  -TWO_PI, pool3[:, dst_sl], ALU.mult, ALU.add)

        redm(ANGc, SIN_Kc)
        sc.wait(ve, ve.n)
        sc.op("activation", pool3[:, SIN_Kc], pool3[:, SIN_Kc], AF.Sin)
        ve.wait(sc, sc.n)
        ve.op("tensor_scalar_add", pool3[:, ANGc], pool3[:, ANGc], PI / 2)
        redm(ANGc, COS_Kc)
        sc.wait(ve, ve.n)
        sc.op("activation", pool3[:, COS_Kc], pool3[:, COS_Kc], AF.Sin)
        ve.wait(sc, sc.n)
        ve.op("tensor_scalar_mul", pool3[:, SIN_Kc], pool3[:, SIN_Kc],
              stats[:, C_SGN:C_SGN + 1])
        ve.op("tensor_scalar_mul", pool3[:, COS_Qc], pool3[:, COS_Kc], ISQ)
        ve.op("tensor_scalar_mul", pool3[:, SIN_Qc], pool3[:, SIN_Kc], ISQ)
        VE_TAB = ve.n



        def rope_proj(wt, wst, src_off, nsrc, row_of, cosT, sinT):
            nonlocal dsl_i
            te.wait(ve, VE_TAB)
            src_big = big1 if src_off is None else big2
            off = 0 if src_off is None else src_off
            loads = [wload(wt[0].rearrange("(c p) e -> p c e", p=128),
                           nsrc * 128),
                     wload(wst[0].rearrange("(c p) e -> p c e", p=128),
                           nsrc * 128)]
            lastr = []
            for oc in range(8):
                if oc + 1 < 8:
                    loads.append(wload(
                        wt[oc + 1].rearrange("(c p) e -> p c e", p=128),
                        nsrc * 128))
                    loads.append(wload(
                        wst[oc + 1].rearrange("(c p) e -> p c e", p=128),
                        nsrc * 128))
                w_v, w_n, w_r = loads[2 * oc]
                ws_v, ws_n, ws_r = loads[2 * oc + 1]
                te.wait(w_n[0], w_n[1])
                te.wait(ws_n[0], ws_n[1])
                for th_ in range(2):
                    tsl = ts(th_, 512)
                    slA = dslots[dsl_i % 4]
                    slB = dslots[(dsl_i + 1) % 4]
                    dsl_i += 2
                    slA.acquire(te)
                    for dc in range(nsrc):
                        te.op_raw("matmul", slA.view,
                                  w_v[:, slice(dc * 128, (dc + 1) * 128)],
                                  src_big[:, slice(off + dc * T + th_ * 512,
                                                   off + dc * T + th_ * 512 + 512)],
                                  start=(dc == 0), stop=(dc == nsrc - 1))
                    slB.acquire(te)
                    for dc in range(nsrc):
                        te.op_raw("matmul", slB.view,
                                  ws_v[:, slice(dc * 128, (dc + 1) * 128)],
                                  src_big[:, slice(off + dc * T + th_ * 512,
                                                   off + dc * T + th_ * 512 + 512)],
                                  start=(dc == 0), stop=(dc == nsrc - 1))
                    te.mark()
                    w_r[0] = te.n
                    ws_r[0] = te.n
                    lastr = [w_r, ws_r]
                    ve.wait(te, te.n)
                    ve.op("tensor_tensor", pool3[:, ANGc][:, 0:512], slA.view,
                          pool3[:, cosT][:, tsl], ALU.mult)
                    ve.op("tensor_tensor", pool3[:, ANGc][:, 512:1024], slB.view,
                          pool3[:, sinT][:, tsl], ALU.mult)
                    slA.consumed(ve)
                    slB.consumed(ve)
                    if pp_done[th_] is not None:
                        ve.wait(*pp_done[th_])
                    ve.op("tensor_tensor", stg[:, tsl],
                          pool3[:, ANGc][:, 0:512], pool3[:, ANGc][:, 512:1024],
                          ALU.add)
                    qb = qpp[qpp_i[0] % 4]
                    qpp_i[0] += 1
                    qb.begin()
                    qb.wait(ve, ve.n)
                    for hh in range(2):
                        h = 2 * oc + hh
                        qb.start(cc_qk_in[slice(row_of(h) + 64, row_of(h) + 128),
                                          slice(th_ * 512, th_ * 512 + 512)],
                                 stg[slice(hh * 64, hh * 64 + 64), tsl])
                    pp_done[th_] = (qb, qb.n)

        rope_proj(rpq_t, rpqs_t, QD_OFF, NL, lambda h: h * 128, COS_Qc, SIN_Qc)
        rope_proj(rpk_t, rpks_t, None, ND, lambda h: 2048 + h * 128,
                  COS_Kc, SIN_Kc)

        vu_v, vu_n, vu_r = wload(vu_w.rearrange("(lc p) e -> p lc e", p=128),
                                 4 * D)
        te.wait(vu_n[0], vu_n[1])
        for i in range(NT):
            for ds_ in range(4):
                sl = dslots[dsl_i % 4]
                dsl_i += 1
                sl.acquire(te)
                for lc in range(NL):
                    te.op_raw("matmul", sl.view,
                              big2[:, slice(lc * T + i * 128, lc * T + (i + 1) * 128)],
                              vu_v[:, slice(lc * D + ds_ * 512,
                                            lc * D + (ds_ + 1) * 512)],
                              start=(lc == 0), stop=(lc == NL - 1))
                te.mark()
                ve.wait(te, te.n)
                if pp_done[ds_ % 2] is not None:
                    ve.wait(*pp_done[ds_ % 2])
                ve.op("tensor_copy", stg[:, ts(ds_ % 2, 512)], sl.view)
                sl.consumed(ve)
                qb = qpp[qpp_i[0] % 4]
                qpp_i[0] += 1
                qb.begin()
                qb.wait(ve, ve.n)
                qb.start(cc_vpush[ts(i, 128), ts(ds_, 512)],
                         stg[:, ts(ds_ % 2, 512)])
                pp_done[ds_ % 2] = (qb, qb.n)
        vu_r[0] = te.n

        if KPHASE <= 4:
            return _early_finish()

        gp.wait(qo, N_CR)
        for qb in qpp:
            gp.wait(qb, qb.n)

        def cc_thunk(kind, ins, outs):
            def th(e, env):
                e.collective_compute(kind, ALU.bypass,
                                     replica_groups=[[0, 1], [2, 3],
                                                     [4, 5], [6, 7]],
                                     ins=ins, outs=outs).then_inc(env["cc"])
            gp.thunks.append(th)
        cc_thunk("AllGather", [cc_cr_in[:, :]], [cc_cr_out[:, :]])
        cc_thunk("AllGather", [cc_qk_in[:, :]], [cc_qk_out[:, :]])
        cc_thunk("AllGather", [cc_vpush[:, :]], [cc_vgath[:, :]])
        CC_QKV = 3

        mb3 = lambda cb: moe_bf[:, cb:cb + 4096].rearrange(
            "p (a b) -> p a b", b=4096)
        MB_E = 32768
        mb_g = lambda e, iq: MB_E + e * 49152 + iq * 4096
        mb_u = lambda e, iq: MB_E + e * 49152 + 16384 + iq * 4096
        mb_d = lambda e, d_: MB_E + e * 49152 + 32768 + d_ * 4096
        cast_blocks = []
        for d_ in range(4):
            for hf_ in range(2):
                cast_blocks.append(
                    (ow_t[d_, hf_ * 1024:(hf_ + 1) * 1024]
                     .rearrange("(hc p) e -> p hc e", p=128),
                     (d_ * 2 + hf_) * 4096))
        for e_ in range(NE):
            for iq_ in range(4):
                cast_blocks.append((eg_t[e_, iq_].rearrange(
                    "(c p) i -> p c i", p=128), mb_g(e_, iq_)))
                cast_blocks.append((eu_t[e_, iq_].rearrange(
                    "(c p) i -> p c i", p=128), mb_u(e_, iq_)))
            for d_ in range(4):
                cast_blocks.append((ed_t[e_, d_].rearrange(
                    "(c p) o -> p c o", p=128), mb_d(e_, d_)))
        TE_CASTOK = te.n
        cast_st = [None] * 4
        cast_q = {}

        def cast_issue(bi):
            src3, _ = cast_blocks[bi]
            j = bi % 4
            q = qws[qw_i[0] % 8]
            qw_i[0] += 1
            gp.wait(q, q.n)
            gp.wait(te, TE_CASTOK)
            if cast_st[j] is not None:
                gp.wait(*cast_st[j])
            n = q.start(big1[:, j * 4096:(j + 1) * 4096].rearrange(
                "p (a b) -> p a b", b=src3.shape[2]), src3)
            cast_q[bi] = (q, n)

        NCB = len(cast_blocks)
        for bi in range(min(3, NCB)):
            cast_issue(bi)
        ow_marks = []
        for bi in range(NCB):
            if bi + 3 < NCB:
                cast_issue(bi + 3)
            j = bi % 4
            qd = qds[j]
            gp.wait(qd, qd.n)
            gp.wait(*cast_q[bi])
            qd.start(moe_bf[:, cast_blocks[bi][1]:cast_blocks[bi][1] + 4096],
                     big1[:, j * 4096:(j + 1) * 4096])
            cast_st[j] = (qd, qd.n)
            if bi == 7:
                ow_marks = [(q_, q_.n) for q_ in qds]
        cast_marks = [(q_, q_.n) for q_ in qds]

        def crread_thunk(e, env):
            e.wait_ge(env["cc"], 1)
            e.dma_start(out=crrow[0:1, :],
                        in_=cc_cr_out[bass.ds(dynv["m_cr"], 1), :]
                        ).then_inc(env["crq"], 16)
        sy.raw(crread_thunk)
        crq.n += 1
        N_CRRD = crq.n
        ve.wait(crq, N_CRRD)
        ve.op("tensor_scalar", crrow[0:1, :], crrow[0:1, :], cmk[0:1, 0:1],
              h0col[0:1, 0:1], ALU.mult, ALU.add)
        VE_CRM = ve.n

        aslots = [p2A, p2B]
        at_done = [0, 0]
        for ec in range(ND):
            qb = qrl[ec % 2]
            qb.begin()
            qb.wait(ve, ve.n)
            if ec < 2:
                qb.wait(qsp[ec % 2], qsp[ec % 2].n)
            qb.start(pool3[:, PH], sc_h[ts(ec, 128), :])
            qb.start(pool3[:, PP], sc_p[ts(ec, 128), :])
            RLQ, ldn = qb, qb.n
            te.wait(ve, VE_CRM)
            p2A.acquire(te)
            te.op_mark("matmul", s4[:, 0:1], crrow[0:1, ts(ec, 128)],
                       onesrow_f[0:1, 0:1], start=True, stop=True)
            ve.wait(te, te.n)
            ve.op("tensor_copy", stats[:, C_CAR:C_CAR + 1], s4[:, 0:1])
            p2A.consumed(ve)
            ve.wait(RLQ, ldn)
            ve.op("scalar_tensor_tensor", pool3[:, PH], pool3[:, PP],
                  stats[:, C_CAR:C_CAR + 1], pool3[:, PH], ALU.mult, ALU.add)
            vea = ve.n
            for i in range(NT):
                sl = aslots[i % 2]
                sl.acquire(te)
                te.wait(ve, vea)
                te.op_mark("transpose", sl.view[:, 0:128],
                           pool3[:, PH][:, ts(i, 128)], ident_f)
                ve.wait(te, te.n)
                ve.wait(qat[i % 2], at_done[i % 2])
                ve.op("tensor_copy", pool3[:, PW][:, ts(i % 2, 128)],
                      sl.view[:, 0:128])
                sl.consumed(ve)
                qb2 = qat[i % 2]
                qb2.wait(ve, ve.n)
                at_done[i % 2] = qb2.start(
                    sc_at[ts(i, 128), ts(ec, 128)],
                    pool3[:, PW][:, ts(i % 2, 128)])
        N_AT = qs.n

        if KPHASE <= 5:
            return _early_finish()

        EXPT0 = 12288
        OST0 = 13312

        def ccwait_qkv(e, env):
            e.wait_ge(env["cc"], CC_QKV)
        sy.raw(ccwait_qkv)
        for rr in range(2):
            for band in range(2):
                cpq.start_dyn(
                    lambda env, rr=rr, band=band:
                        myqk[rr * 2048 + band * 1024: rr * 2048 + band * 1024 + 1024, :],
                    lambda env, rr=rr, band=band:
                        cc_qk_out[rr * 4096 + band * 2048:, :][
                            bass.ds(dynv["m_qk"], 1024), :], tag=f"cqk{rr}{band}")
            cpq.start_dyn(
                lambda env, rr=rr: myv[rr * 1024: rr * 1024 + 1024, :],
                lambda env, rr=rr:
                    cc_vgath[rr * 1024:, :][bass.ds(dynv["m_vr"], 1024),
                                            bass.ds(dynv["m_vc"], 1024)],
                tag=f"cv{rr}")
        N_COMPACT = cpq.n

        def head_load(h, slot):
            base = slot * 6144
            q = qhs[slot]
            q.begin()
            sy.wait(cpq, N_COMPACT)
            q.wait(te, te.n)
            q.wait(ve, ve.n)
            q.wait(sc, sc.n)
            for rr in range(2):
                q.start(big2[:, slice(base + rr * 1024, base + rr * 1024 + 1024)],
                        myqk[slice(rr * 2048 + 1024 + h * 128,
                                   rr * 2048 + 1024 + (h + 1) * 128), :])
                q.start(big2[:, slice(base + 2048 + rr * 1024,
                                      base + 2048 + rr * 1024 + 1024)],
                        myqk[slice(rr * 2048 + h * 128,
                                   rr * 2048 + (h + 1) * 128), :])
                q.start(big2[:, slice(base + 4096 + rr * 1024,
                                      base + 4096 + rr * 1024 + 1024)].rearrange(
                            "p (tb dv) -> p tb dv", dv=128),
                        myv[rr * 1024:(rr + 1) * 1024,
                            h * 128:(h + 1) * 128].rearrange(
                            "(tb p) dv -> p tb dv", p=128))
            return (q, q.n)

        ld_head = [None] * NH_LOC
        ld_head[0] = head_load(0, 0)
        ve.wait(*pp_done[0])
        ve.wait(*pp_done[1])
        ve.wait(qat[0], at_done[0])
        ve.wait(qat[1], at_done[1])
        o_done = [None, None]
        sT = [p0A, p0B]
        avs = [p1A, p1B]
        ex_rel = [0, 0]
        for hl in range(NH_LOC):
            slot = hl % 2
            base = slot * 6144
            Ksb = big2[:, slice(base, base + 2048)]
            Qsb = big2[:, slice(base + 2048, base + 4096)]
            Vsb = big2[:, slice(base + 4096, base + 6144)]
            if hl + 1 < NH_LOC:
                ld_head[hl + 1] = head_load(hl + 1, (hl + 1) % 2)
            ve.wait(ld_head[hl][0], ld_head[hl][1])
            ve.op("tensor_tensor", stg[:, 0:2048], Ksb, Ksb, ALU.mult)
            veksq = ve.n
            te.wait(ve, veksq)
            for half in range(2):
                p2B.acquire(te)
                te.op_raw("matmul", s5[0:1, :], onescol_b[:, 0:1],
                          stg[:, slice(half * 1024, half * 1024 + 512)],
                          start=True, stop=True)
                p2A.acquire(te)
                te.op_mark("matmul", s4[0:1, :], onescol_b[:, 0:1],
                           stg[:, slice(half * 1024 + 512, half * 1024 + 1024)],
                           start=True, stop=True)
                tk = te.n
                ve.wait(te, tk)
                ve.op("tensor_copy", k2row, s5[0:1, :])
                ve.op("tensor_reduce", mxcell if half else mxcell,
                      k2row, AX.X, ALU.max) if False else None
                ve.op("tensor_reduce", rowpool[0:1, 3585:3586], k2row, AX.X,
                      ALU.max)
                ve.op("tensor_copy", k2row, s4[0:1, :])
                ve.op("tensor_reduce", rowpool[0:1, 3586:3587], k2row, AX.X,
                      ALU.max)
                if half == 0:
                    ve.op("tensor_tensor", mxcell, rowpool[0:1, 3585:3586],
                          rowpool[0:1, 3586:3587], ALU.max)
                else:
                    ve.op("tensor_tensor", rowpool[0:1, 3585:3586],
                          rowpool[0:1, 3585:3586], rowpool[0:1, 3586:3587],
                          ALU.max)
                    ve.op("tensor_tensor", mxcell, mxcell,
                          rowpool[0:1, 3585:3586], ALU.max)
                p2A.consumed(ve)
                p2B.consumed(ve)
            ve.op("tensor_tensor", stg[:, 0:2048], Qsb, Qsb, ALU.mult)
            veqsq = ve.n
            te.wait(ve, veqsq)
            for half in range(2):
                p2B.acquire(te)
                te.op_raw("matmul", s5[0:1, :], onescol_b[:, 0:1],
                          stg[:, slice(half * 1024, half * 1024 + 512)],
                          start=True, stop=True)
                p2A.acquire(te)
                te.op_mark("matmul", s4[0:1, :], onescol_b[:, 0:1],
                           stg[:, slice(half * 1024 + 512, half * 1024 + 1024)],
                           start=True, stop=True)
                tq = te.n
                ve.wait(te, tq)
                ve.op("tensor_copy", q2row[0:1, slice(half * 1024, half * 1024 + 512)],
                      s5[0:1, :])
                ve.op("tensor_copy", q2row[0:1, slice(half * 1024 + 512,
                                                      half * 1024 + 1024)],
                      s4[0:1, :])
                p2A.consumed(ve)
                p2B.consumed(ve)
            ve.op("tensor_scalar_mul", q2row, q2row, mxcell)
            veq2 = ve.n
            sc.wait(ve, veq2)
            sc.op("activation", q2row, q2row, AF.Ln, bias=stats[0:1, C_E12:C_E12 + 1])
            sc.op("activation", q2row, q2row, AF.Exp, scale=0.5)
            ve.wait(sc, sc.n)
            ve.op("tensor_scalar", nbrow, q2row, -1.0, BSHIFT,
                  ALU.mult, ALU.add)
            venb = ve.n

            for qc in range(4):
                qbase = qc * 512
                av = avs[qc % 2]
                av.acquire(te)
                nkt = 4 * (qc + 1)
                test = {}
                exq = {}

                def emit_scores(kt):
                    qlo = max(qbase, kt * 128)
                    qw = qbase + 512 - qlo
                    stv = sT[kt % 2]
                    stv.acquire(te)
                    te.wait(ve, venb)
                    te.op_raw("matmul", stv.view[:, 0:qw], Ksb[:, ts(kt, 128)],
                              Qsb[:, slice(qlo, qbase + 512)],
                              start=True, stop=False)
                    te.op_raw("matmul", stv.view[:, 0:qw], onesrow_b[0:1, 0:128],
                              nbrow[0:1, slice(qlo, qbase + 512)],
                              start=False, stop=True)
                    te.mark()
                    test[kt] = te.n

                def emit_exp(kt):
                    qlo = max(qbase, kt * 128)
                    qw = qbase + 512 - qlo
                    stv = sT[kt % 2]
                    eslot = EXPT0 + 512 * (kt % 2)
                    ex = big2[:, slice(eslot, eslot + qw)]
                    sc.wait(te, test[kt])
                    sc.wait(te, ex_rel[kt % 2])
                    sc.op("activation", ex, stv.view[:, 0:qw], AF.Exp)
                    stv.consumed(sc)
                    dq = kt * 128 - qlo
                    if 0 <= dq < qw:
                        mw = min(128, qw - dq)
                        ve.wait(sc, sc.n)
                        ve.op("tensor_tensor", ex[:, dq:dq + mw],
                              ex[:, dq:dq + mw], causal[:, 0:mw], ALU.mult)
                        exq[kt] = (ve, ve.n)
                    else:
                        exq[kt] = (sc, sc.n)

                def emit_av(kt):
                    qlo = max(qbase, kt * 128)
                    qw = qbase + 512 - qlo
                    eslot = EXPT0 + 512 * (kt % 2)
                    ex = big2[:, slice(eslot, eslot + qw)]
                    eng, n_ = exq[kt]
                    te.wait(eng, n_)
                    if kt == 0:
                        p2A.acquire(te)
                    te.op_raw("matmul", av.view[:, slice(qlo - qbase, 512)],
                              Vsb[:, ts(kt, 128)], ex,
                              start=(kt == 0), stop=(kt == nkt - 1))
                    te.op_raw("matmul", s4[0:1, slice(qlo - qbase, 512)],
                              onescol_b[:, 0:1], ex,
                              start=(kt == 0), stop=(kt == nkt - 1))
                    te.mark()
                    ex_rel[kt % 2] = te.n

                emit_scores(0)
                emit_exp(0)
                for kt in range(nkt):
                    if kt + 1 < nkt:
                        emit_scores(kt + 1)
                        emit_exp(kt + 1)
                    emit_av(kt)
                teav = te.n
                ve.wait(te, teav)
                ve.op("tensor_copy", denrow, s4[0:1, :])
                p2A.consumed(ve)
                sc.wait(ve, ve.n)
                sc.op("activation", denrow, denrow, AF.Ln)
                sc.op("activation", denrow, denrow, AF.Exp, scale=-1.0)
                te.wait(sc, sc.n)
                p2B.acquire(te)
                te.op_mark("matmul", s5[:, :], onesrow_f[0:1, :],
                           denrow, start=True, stop=True)
                tebc = te.n
                ve.wait(te, tebc)
                ve.op("tensor_copy", pool3[:, PBC][:, 0:512], s5[:, :])
                p2B.consumed(ve)
                ost = OST0 + 512 * (qc % 2)
                if o_done[qc % 2] is not None:
                    ve.wait(*o_done[qc % 2])
                ve.op("tensor_tensor", big2[:, slice(ost, ost + 512)],
                      av.view, pool3[:, PBC][:, 0:512], ALU.mult)
                av.consumed(ve)
                veo = ve.n
                qp_ = qsp[qc % 2]
                qp_.wait(ve, veo)
                o_done[qc % 2] = (qp_, qp_.start(
                    cc_o_in[slice(hl * 128, hl * 128 + 128), ts(qc, 512)],
                    big2[:, slice(ost, ost + 512)]))

        gp.wait(qsp[0], qsp[0].n)
        gp.wait(qsp[1], qsp[1].n)
        cc_thunk("AllGather", [cc_o_in[:, :]], [cc_o_out[:, :]])
        CC_O = 4

        if KPHASE <= 6:
            return _early_finish()

        def ccwait_o(e, env):
            e.wait_ge(env["cc"], CC_O)
        sy.raw(ccwait_o)
        qs.wait(te, te.n)
        qs.wait(ve, ve.n)
        qs.wait(sc, sc.n)
        for rk in range(2):
            ocq.start_dyn(
                lambda env, rk=rk: myo[rk * 1024:(rk + 1) * 1024, :],
                lambda env, rk=rk: cc_o_out[rk * 1024:, :][
                    bass.ds(dynv["m_or"], 1024),
                    bass.ds(dynv["m_oc"], 1024)], tag=f"cO{rk}")
        N_OC = ocq.n
        sy.wait(ocq, N_OC)
        for h in range(16):
            orq.start(big2[:, ts(h, 1024)],
                      myo[slice((h // 8) * 1024 + (h % 8) * 128,
                                (h // 8) * 1024 + (h % 8) * 128 + 128), :])
        N_ORD = orq.n

        te.wait(orq, N_ORD)
        dsl_i = 0
        ring_align(4096)
        for m_ in ow_marks:
            gp.wait(*m_)
        owl = [wload(mb3(0), 4096), wload(mb3(4096), 4096)]
        for dsl in range(4):
            if dsl + 1 < 4:
                owl.append(wload(mb3((dsl + 1) * 2 * 4096), 4096))
            vA, nA, rA = owl[2 * dsl]
            vB, nB, rB = owl[2 * dsl + 1]
            te.wait(nA[0], nA[1])
            te.wait(nB[0], nB[1])
            for i in range(NT):
                sl = dslots[dsl_i % 4]
                dsl_i += 1
                sl.acquire(te)
                for hc in range(ND):
                    vv = vA if hc < 8 else vB
                    te.op_raw("matmul", sl.view,
                              big2[:, slice(hc * 1024 + i * 128,
                                            hc * 1024 + (i + 1) * 128)],
                              vv[:, ts(hc % 8, 512)],
                              start=(hc == 0), stop=(hc == ND - 1))
                te.mark()
                ve.wait(te, te.n)
                ve.op("tensor_tensor",
                      xh[:, slice(i * D + dsl * 512, i * D + dsl * 512 + 512)],
                      xh[:, slice(i * D + dsl * 512, i * D + dsl * 512 + 512)],
                      sl.view, ALU.add)
                sl.consumed(ve)
            rA[0] = rB[0] = te.n
            if dsl + 1 < 4:
                owl.append(wload(mb3(((dsl + 1) * 2 + 1) * 4096), 4096))
        VE_H = ve.n

        crq.begin()
        crq.wait(te, te.n)
        crq.wait(ve, ve.n)
        crq.wait(sc, sc.n)
        crq.start(ln2row_sb, ln2_row[:, :])
        N_LN2 = crq.n
        te.wait(crq, N_LN2)
        for j, sl in enumerate([p0A, p0B, p1A, p1B]):
            sl.acquire(te)
            te.op_mark("matmul", sl.view, onesrow_f[0:1, :],
                       ln2row_sb[0:1, ts(j, 512)], start=True, stop=True)
            ve.wait(te, te.n)
            ve.op("tensor_copy", pool3[:, PLN2][:, ts(j, 512)], sl.view)
            sl.consumed(ve)

        for m_ in cast_marks:
            ve.wait(*m_)
        for i in range(NT):
            ht = xh[:, ts(i, D)]
            ve.wait(ve, VE_H)
            ve.op("tensor_tensor", pool3[:, PSQ], ht, ht, ALU.mult)
            ve.op("tensor_reduce", stats[:, C_TMP:C_TMP + 1], pool3[:, PSQ],
                  AX.X, ALU.add)
            ve.op("tensor_scalar_mul", stats[:, C_TMP:C_TMP + 1],
                  stats[:, C_TMP:C_TMP + 1], 1.0 / D)
            sc.wait(ve, ve.n)
            sc.op("activation", stats[:, C_TMP + 1:C_TMP + 2],
                  stats[:, C_TMP:C_TMP + 1], AF.Ln, bias=stats[:, C_EPS:C_EPS + 1])
            sc.op("activation", stats[:, C_RS2 + i:C_RS2 + i + 1],
                  stats[:, C_TMP + 1:C_TMP + 2], AF.Exp, scale=-0.5)
            ve.wait(sc, sc.n)
            ve.op("scalar_tensor_tensor", big1[:, ts(i, D)], ht,
                  stats[:, C_RS2 + i:C_RS2 + i + 1], pool3[:, PLN2],
                  ALU.mult, ALU.mult)
        VE_RMS2TM = ve.n

        tp = 0
        te.wait(ve, VE_RMS2TM)
        for ec in range(ND):
            for i in range(NT):
                sl = tslots[tp % 2]
                sl.acquire(te)
                te.op_mark("transpose", sl.view,
                           big1[:, slice(i * D + ec * 128, i * D + (ec + 1) * 128)],
                           ident_b)
                ve.wait(te, te.n)
                ve.op("tensor_copy",
                      big2[:, slice(ec * T + i * 128, ec * T + (i + 1) * 128)],
                      sl.view)
                sl.consumed(ve)
                tp += 1
        VE_RMS2FM = ve.n

        rw_v, rw_n, rw_r = wload(routw.rearrange("(dc p) e -> p dc e", p=128),
                                 ND * 7)
        te.wait(rw_n[0], rw_n[1])
        te.wait(ve, VE_RMS2FM)
        for i in range(NT):
            sl = [p2A, p2B][i % 2]
            sl.acquire(te)
            for dc in range(ND):
                te.op_raw("matmul", sl.view[:, 0:7],
                          big2[:, slice(dc * T + i * 128, dc * T + (i + 1) * 128)],
                          rw_v[:, slice(dc * 7, dc * 7 + 7)],
                          start=(dc == 0), stop=(dc == ND - 1))
            te.mark()
            ve.wait(te, te.n)
            ve.op("tensor_copy", stats[:, C_WRT + 7 * i:C_WRT + 7 * i + 7],
                  sl.view[:, 0:7])
            sl.consumed(ve)
        rw_r[0] = te.n
        p2A.acquire(te)
        te.op_mark("matmul", s4[:, 0:7], onesrow_f[0:1, :], routb_sb[0:1, 0:7],
                   start=True, stop=True)
        TE_RB = te.n
        ve.wait(te, TE_RB)
        ve.op("tensor_copy", pool3[:, PW][:, 0:7], s4[:, 0:7])
        p2A.consumed(ve)
        for i in range(NT):
            lg = stats[:, C_WRT + 7 * i:C_WRT + 7 * i + 7]
            W8, W16, W24 = pool3[:, PW][:, 8:15], pool3[:, PW][:, 16:23], \
                pool3[:, PW][:, 24:31]
            t0 = stats[:, C_TMP:C_TMP + 1]
            t1 = stats[:, C_TMP + 1:C_TMP + 2]
            t2 = stats[:, C_TMP + 2:C_TMP + 3]
            t3 = stats[:, C_TMP + 3:C_TMP + 4]
            t4 = stats[:, C_TMP + 4:C_TMP + 5]
            ve.op("tensor_tensor", lg, lg, pool3[:, PW][:, 0:7], ALU.add)
            ve.op("tensor_reduce", t0, lg, AX.X, ALU.max)
            ve.op("tensor_scalar", W8, lg, t0, None, ALU.is_ge)
            ve.op("scalar_tensor_tensor", W16, W8, -1e30, lg, ALU.mult, ALU.add)
            ve.op("tensor_reduce", t1, W16, AX.X, ALU.max)
            ve.op("tensor_scalar", W24, W16, t1, None, ALU.is_ge)
            sc.wait(ve, ve.n)
            sc.op("activation", t2, t0, AF.Exp, scale=-1.0)
            sc.op("activation", t3, t1, AF.Exp, scale=-1.0)
            ve.wait(sc, sc.n)
            ve.op("tensor_scalar_add", t2, t2, 1.0)
            ve.op("tensor_scalar_add", t3, t3, 1.0)
            sc.wait(ve, ve.n)
            sc.op("activation", t2, t2, AF.Ln)
            sc.op("activation", t2, t2, AF.Exp, scale=-1.0)
            sc.op("activation", t3, t3, AF.Ln)
            sc.op("activation", t3, t3, AF.Exp, scale=-1.0)
            ve.wait(sc, sc.n)
            ve.op("tensor_tensor", t4, t2, t3, ALU.add)
            sc.wait(ve, ve.n)
            sc.op("activation", t4, t4, AF.Ln)
            sc.op("activation", t4, t4, AF.Exp, scale=-1.0)
            ve.wait(sc, sc.n)
            ve.op("tensor_scalar_mul", W8, W8, t2)
            ve.op("tensor_scalar_mul", W24, W24, t3)
            ve.op("tensor_tensor", lg, W8, W24, ALU.add)
            ve.op("tensor_scalar_mul", lg, lg, t4)
        VE_ROUTE = ve.n

        if KPHASE <= 7:
            return _early_finish()

        ring_align(4096)
        silu_rel = [0, 0]
        he_rel = [0] * 8
        for m_ in cast_marks:
            gp.wait(*m_)
        for e_i in range(NE):
            gl = [wload(mb3(mb_g(e_i, 0)), 4096),
                  wload(mb3(mb_u(e_i, 0)), 4096)]
            te.wait(ve, VE_RMS2FM)
            for iq in range(4):
                if iq + 1 < 4:
                    gl.append(wload(mb3(mb_g(e_i, iq + 1)), 4096))
                g_v, g_n, g_r = gl[2 * iq]
                u_v, u_n, u_r = gl[2 * iq + 1]
                te.wait(g_n[0], g_n[1])
                te.wait(u_n[0], u_n[1])
                for j in range(2):
                    ic = iq * 2 + j
                    for th_ in range(2):
                        sl = [p0A, p0B][th_]
                        sl.acquire(te)
                        for dc in range(ND):
                            te.op_raw("matmul", sl.view,
                                      g_v[:, slice(dc * 256 + j * 128,
                                                   dc * 256 + j * 128 + 128)],
                                      big2[:, slice(dc * T + th_ * 512,
                                                    dc * T + th_ * 512 + 512)],
                                      start=(dc == 0), stop=(dc == ND - 1))
                        te.mark()
                        sc.wait(te, te.n)
                        sc.wait(ve, silu_rel[ic % 2])
                        sc.op("activation",
                              hE[:, slice((ic % 2) * 1024 + th_ * 512,
                                          (ic % 2) * 1024 + th_ * 512 + 512)],
                              sl.view, AF.Silu)
                        sl.consumed(sc)
                    for th_ in range(2):
                        sl = [p1A, p1B][th_]
                        sl.acquire(te)
                        for dc in range(ND):
                            te.op_raw("matmul", sl.view,
                                      u_v[:, slice(dc * 256 + j * 128,
                                                   dc * 256 + j * 128 + 128)],
                                      big2[:, slice(dc * T + th_ * 512,
                                                    dc * T + th_ * 512 + 512)],
                                      start=(dc == 0), stop=(dc == ND - 1))
                        te.mark()
                        ve.wait(te, te.n)
                        ve.wait(te, he_rel[ic])
                        ve.wait(sc, sc.n)
                        ve.op("scalar_tensor_tensor",
                              big1[:, slice(ic * 1024 + th_ * 512,
                                            ic * 1024 + th_ * 512 + 512)],
                              sl.view, 0.125,
                              hE[:, slice((ic % 2) * 1024 + th_ * 512,
                                          (ic % 2) * 1024 + th_ * 512 + 512)],
                              ALU.mult, ALU.mult)
                        silu_rel[ic % 2] = ve.n
                        sl.consumed(ve)
                g_r[0] = u_r[0] = te.n
                if iq + 1 < 4:
                    gl.append(wload(mb3(mb_u(e_i, iq + 1)), 4096))
            VE_HE = ve.n
            dl = [wload(mb3(mb_d(e_i, 0)), 4096)]
            for dsl in range(4):
                if dsl + 1 < 4:
                    dl.append(wload(mb3(mb_d(e_i, dsl + 1)), 4096))
                d_v, d_n, d_r = dl[dsl]
                te.wait(d_n[0], d_n[1])
                te.wait(ve, VE_HE)
                for tc in range(8):
                    sl = [p2A, p2B][tc % 2]
                    sl.acquire(te)
                    for ic in range(NI):
                        te.op_raw("matmul", sl.view,
                                  big1[:, slice(ic * 1024 + tc * 128,
                                                ic * 1024 + tc * 128 + 128)],
                                  d_v[:, ts(ic, 512)],
                                  start=(ic == 0), stop=(ic == NI - 1))
                    te.mark()
                    wc = (stats[:, C_ONE:C_ONE + 1] if e_i == 0 else
                          stats[:, C_WRT + 7 * tc + (e_i - 1):
                                C_WRT + 7 * tc + e_i])
                    dst = xh[:, slice(tc * D + dsl * 512,
                                      tc * D + dsl * 512 + 512)]
                    ve.wait(te, te.n)
                    ve.wait(ve, VE_ROUTE)
                    ve.op("scalar_tensor_tensor", dst, sl.view, wc, dst,
                          ALU.mult, ALU.add)
                    sl.consumed(ve)
                d_r[0] = te.n
            for ic in range(8):
                he_rel[ic] = te.n
        VE_MOE = ve.n

        xfv = big1.bitcast(F32)
        TE_P8 = te.n
        ve_read = [0, 0]
        st_done = [qsp[0].n, qsp[1].n]
        for i in range(NT):
            s = i % 2
            xsl = slice(s * 4096, s * 4096 + 2048)
            asl = slice(s * 4096 + 2048, s * 4096 + 4096)
            qb = qrl[s]
            qb.begin()
            qb.wait(te, TE_P8)
            qb.wait(ve, ve_read[s])
            qb.wait(qsp[s], st_done[s])
            qb.wait(qat[0], at_done[0])
            qb.wait(qat[1], at_done[1])
            qb.start(xfv[:, xsl], x_in[ts(i, 128), :])
            qb.start(xfv[:, asl], sc_at[ts(i, 128), :])
            ve.wait(qb, qb.n)
            ve.op("tensor_tensor", xfv[:, xsl], xfv[:, xsl],
                  xfv[:, asl], ALU.add)
            ve.op("tensor_tensor", xfv[:, xsl], xfv[:, xsl],
                  xh[:, ts(i, D)], ALU.add)
            ve_read[s] = ve.n
            qsp[s].wait(ve, ve.n)
            st_done[s] = qsp[s].start(out_hbm[ts(i, 128), :], xfv[:, xsl])

        SP_FINAL = [16 * qsp[0].n, 16 * qsp[1].n]

        def final_thunk(e, env):
            e.wait_ge(env["sp0"], SP_FINAL[0])
            e.wait_ge(env["sp1"], SP_FINAL[1])
        sy.raw(final_thunk)

        _emit()
    return nc


_NC_CACHE = {}


def make_in_maps(L0_attn_x, L0_attn_Wz, L0_attn_bz, L0_attn_Wn, L0_attn_bn,
           L0_ffn_ln1_w, L0_ffn_ln2_w, L0_ffn_kv_d_w, L0_ffn_q_d_w,
           L0_ffn_k_u_w, L0_ffn_v_u_w, L0_ffn_q_u_w, L0_ffn_rope_k_w,
           L0_ffn_rope_q_w, L0_ffn_o_w, L0_ffn_sh_gate, L0_ffn_sh_up,
           L0_ffn_sh_down, L0_ffn_r_gate, L0_ffn_r_up, L0_ffn_r_down,
           L0_ffn_router_w, L0_ffn_router_b):
    f = np.asarray
    x = np.ascontiguousarray(f(L0_attn_x), dtype=np.float32)
    Wz = f(L0_attn_Wz); Wn = f(L0_attn_Wn)
    cvt = lambda a: np.ascontiguousarray(np.asarray(a, dtype=np.float32))

    wzt = cvt(Wz.reshape(ND, 128, D).transpose(0, 2, 1))
    wnt = cvt(Wn.reshape(ND, 128, D).transpose(0, 2, 1))
    bz_r = cvt(f(L0_attn_bz).reshape(ND, 128).T)
    bn_r = cvt(f(L0_attn_bn).reshape(ND, 128).T)
    ln1_r = cvt(f(L0_ffn_ln1_w).reshape(ND, 128).T)
    ln2_row = cvt(f(L0_ffn_ln2_w).reshape(1, D))
    kvd_t = cvt(f(L0_ffn_kv_d_w).reshape(D, NL, 128).transpose(1, 0, 2))
    qd_t = cvt(f(L0_ffn_q_d_w).reshape(D, NL, 128).transpose(1, 0, 2))
    ku_w = cvt(L0_ffn_k_u_w)
    qu_w = cvt(L0_ffn_q_u_w)
    vu_w = cvt(L0_ffn_v_u_w)
    perm = np.concatenate([np.concatenate([np.arange(32, 64), np.arange(0, 32)])
                           + 64 * h for h in range(16)])
    rpk = f(L0_ffn_rope_k_w)
    rpq = f(L0_ffn_rope_q_w)
    rpk_t = cvt(rpk.reshape(D, 8, 128).transpose(1, 0, 2))
    rpks_t = cvt(rpk[:, perm].reshape(D, 8, 128).transpose(1, 0, 2))
    rpq_t = cvt(rpq.reshape(LAT, 8, 128).transpose(1, 0, 2))
    rpqs_t = cvt(rpq[:, perm].reshape(LAT, 8, 128).transpose(1, 0, 2))
    ow_t = cvt(f(L0_ffn_o_w).reshape(D, 4, 512).transpose(1, 0, 2))
    routw = cvt(L0_ffn_router_w)
    routb = cvt(f(L0_ffn_router_b).reshape(1, 7))
    eg0 = np.concatenate([f(L0_ffn_sh_gate)[None], f(L0_ffn_r_gate)], 0)
    eu0 = np.concatenate([f(L0_ffn_sh_up)[None], f(L0_ffn_r_up)], 0)
    eg_t = cvt(eg0.reshape(NE, D, 4, 256).transpose(0, 2, 1, 3))
    eu_t = cvt(eu0.reshape(NE, D, 4, 256).transpose(0, 2, 1, 3))
    ed = np.concatenate([f(L0_ffn_sh_down)[None], f(L0_ffn_r_down)], 0)
    ed_t = cvt(ed.reshape(NE, IM, 4, 512).transpose(0, 2, 1, 3))

    shared = dict(wzt=wzt, wnt=wnt, bz_r=bz_r, bn_r=bn_r, ln1_r=ln1_r,
                  ln2_row=ln2_row, kvd_t=kvd_t, qd_t=qd_t, ku_w=ku_w,
                  qu_w=qu_w, vu_w=vu_w, rpk_t=rpk_t, rpks_t=rpks_t,
                  rpq_t=rpq_t, rpqs_t=rpqs_t, ow_t=ow_t, routw=routw,
                  routb=routb, eg_t=eg_t, eu_t=eu_t, ed_t=ed_t)

    in_maps = []
    for c in range(8):
        b, half = c // 2, c % 2
        hoff = half * 8
        m = np.zeros((1, 8), np.int32)
        m[0, 0] = hoff * 128
        m[0, 1] = 0
        m[0, 2] = hoff * 128
        m[0, 3] = 0
        m[0, 4] = half * 1024
        m[0, 5] = 0
        im = dict(shared)
        im["x"] = np.ascontiguousarray(x[b, half * T:(half + 1) * T, :])
        im["pos"] = np.arange(half * T, (half + 1) * T,
                              dtype=np.float32).reshape(1, T)
        im["h0"] = np.full((1, 1), 0.0 if half else 0.5, np.float32)
        im["cmask"] = np.full((1, 1), 1.0 if half else 0.0, np.float32)
        im["meta"] = m
        m[0, 0] = hoff * 128
        m[0, 2] = hoff * 128
        in_maps.append(im)

    return in_maps


def kernel(**inputs):
    in_maps = make_in_maps(**inputs)
    nc = build_nc()
    res = run_bass_kernel_spmd(nc, in_maps, core_ids=list(range(8)))
    out = np.empty((4, 2048, D), np.float32)
    for c in range(8):
        b, half = c // 2, c % 2
        out[b, half * T:(half + 1) * T, :] = res.results[c]["out"]
    return out


if __name__ == "__main__":
    pass


# version 12 (final)
# speedup vs baseline: 1.0239x; 1.0239x over previous
import os
import numpy as np
import concourse.bass as bass
import concourse.mybir as mybir
from concourse.bass_utils import run_bass_kernel_spmd

F32 = mybir.dt.float32
BF16 = mybir.dt.bfloat16
I32 = mybir.dt.int32
AF = mybir.ActivationFunctionType
ALU = mybir.AluOpType
AX = mybir.AxisListType

PI = float(np.pi)
TWO_PI = float(2 * np.pi)
LN1E4_32 = float(np.log(10000.0) / 32.0)

T = 1024
NT = 8
D = 2048
ND = 16
LAT = 512
NL = 4
NH_LOC = 8
S2 = 2048
IM = 1024
NI = 8
NE = 8
EPS = 1e-5
ISQ = float(1.0 / np.sqrt(128.0))
BSHIFT = 40.0


class _Eng:

    def __init__(self, name, auto):
        self.name = name
        self.auto = auto
        self.n = 0
        self.thunks = []
        self._sat = {}

    def wait(self, other, val=None):
        if isinstance(other, _Eng):
            semname, v = other.name, (other.n if val is None else val)
        else:
            semname, v = other.name, 16 * (other.n if val is None else val)
        if v <= 0 or v <= self._sat.get(semname, -1):
            return
        self._sat[semname] = v

        def th(e, env):
            e.wait_ge(env[semname], v)
        self.thunks.append(th)

    def op(self, fname, *args, **kwargs):
        if self.auto:
            self.wait(self, self.n)
        myname, auto = self.name, self.auto

        def th(e, env):
            inst = getattr(e, fname)(*args, **kwargs)
            if auto:
                inst.then_inc(env[myname], 1)
        self.thunks.append(th)
        if self.auto:
            self.n += 1
        return self.n

    def op_mark(self, fname, *args, **kwargs):
        myname = self.name

        def th(e, env):
            getattr(e, fname)(*args, **kwargs).then_inc(env[myname], 1)
        self.thunks.append(th)
        self.n += 1
        return self.n

    def op_raw(self, fname, *args, **kwargs):
        cell = [False]
        myname = self.name

        def th(e, env):
            inst = getattr(e, fname)(*args, **kwargs)
            if cell[0]:
                inst.then_inc(env[myname], 1)
        self.thunks.append(th)
        self._last_raw = cell

    def mark(self):
        self._last_raw[0] = True
        self._last_raw = None
        self.n += 1
        return self.n

    def raw(self, fn):
        self.thunks.append(fn)


class _Dma:
    def __init__(self, eng, name):
        self.eng = eng
        self.name = name
        self.n = 0

    def wait(self, other, val=None):
        self.eng.wait(other, val)

    def begin(self):
        self.eng.wait(self, self.n)

    def start(self, out, in_, **kw):
        myname = self.name

        def th(e, env):
            e.dma_start(out=out, in_=in_, **kw).then_inc(env[myname], 16)
        self.eng.thunks.append(th)
        self.n += 1
        return self.n

    def start_dyn(self, outf, inf, tag="", **kw):
        myname = self.name

        def th(e, env):
            try:
                e.dma_start(out=outf(env), in_=inf(env), **kw).then_inc(
                    env[myname], 16)
            except Exception:
                raise RuntimeError(f"start_dyn failed tag={tag}")
        self.eng.thunks.append(th)
        self.n += 1
        return self.n


class _Slot:

    def __init__(self, view):
        self.view = view
        self.cons = []

    def acquire(self, te):
        for eng, cnt in self.cons:
            te.wait(eng, cnt)
        self.cons = []

    def consumed(self, eng):
        self.cons.append((eng, eng.n))


def ts(i, sz):
    return slice(i * sz, (i + 1) * sz)


KPHASE = int(os.environ.get("KPHASE", "9"))
KGRU = int(os.environ.get("KGRU", "9"))


def build_nc():
    nc = bass.Bass("TRN2", target_bir_lowering=False)
    dt = nc.dram_tensor

    x_in = dt("x", [T, D], F32, kind="ExternalInput").ap()
    pos_in = dt("pos", [1, T], F32, kind="ExternalInput").ap()
    h0_in = dt("h0", [1, 1], F32, kind="ExternalInput").ap()
    cmask_in = dt("cmask", [1, 1], F32, kind="ExternalInput").ap()
    meta_in = dt("meta", [1, 8], I32, kind="ExternalInput").ap()

    wzt = dt("wzt", [ND, D, 128], F32, kind="ExternalInput").ap()
    wnt = dt("wnt", [ND, D, 128], F32, kind="ExternalInput").ap()
    bz_r = dt("bz_r", [128, ND], F32, kind="ExternalInput").ap()
    bn_r = dt("bn_r", [128, ND], F32, kind="ExternalInput").ap()
    ln1_r = dt("ln1_r", [128, ND], F32, kind="ExternalInput").ap()
    ln2_row = dt("ln2_row", [1, D], F32, kind="ExternalInput").ap()
    kvd_t = dt("kvd_t", [NL, D, 128], F32, kind="ExternalInput").ap()
    qd_t = dt("qd_t", [NL, D, 128], F32, kind="ExternalInput").ap()
    ku_w = dt("ku_w", [LAT, 1024], F32, kind="ExternalInput").ap()
    qu_w = dt("qu_w", [LAT, 1024], F32, kind="ExternalInput").ap()
    vu_w = dt("vu_w", [LAT, D], F32, kind="ExternalInput").ap()
    rpk_t = dt("rpk_t", [8, D, 128], F32, kind="ExternalInput").ap()
    rpks_t = dt("rpks_t", [8, D, 128], F32, kind="ExternalInput").ap()
    rpq_t = dt("rpq_t", [8, LAT, 128], F32, kind="ExternalInput").ap()
    rpqs_t = dt("rpqs_t", [8, LAT, 128], F32, kind="ExternalInput").ap()
    ow_t = dt("ow_t", [4, D, 512], F32, kind="ExternalInput").ap()
    routw = dt("routw", [D, 7], F32, kind="ExternalInput").ap()
    routb = dt("routb", [1, 7], F32, kind="ExternalInput").ap()
    eg_t = dt("eg_t", [NE, 4, D, 256], F32, kind="ExternalInput").ap()
    eu_t = dt("eu_t", [NE, 4, D, 256], F32, kind="ExternalInput").ap()
    ed_t = dt("ed_t", [NE, 4, IM, 512], F32, kind="ExternalInput").ap()

    out_hbm = dt("out", [T, D], F32, kind="ExternalOutput").ap()

    cc_qk_in = dt("cc_qk_in", [4096, T], BF16).ap()
    cc_qk_out = dt("cc_qk_out", [2 * 4096, T], BF16).ap()
    cc_vpush = dt("cc_vpush", [T, D], BF16).ap()
    cc_vgath = dt("cc_vgath", [2 * T, D], BF16).ap()
    cc_cr_in = dt("cc_cr_in", [1, D], F32).ap()
    cc_cr_out = dt("cc_cr_out", [2, D], F32).ap()
    cc_o_in = dt("cc_o_in", [1024, S2], BF16).ap()
    cc_o_out = dt("cc_o_out", [2 * 1024, S2], BF16).ap()
    myqk = dt("myqk", [4096, T], BF16).ap()
    myv = dt("myv", [2048, T], BF16).ap()
    myo = dt("myo", [2048, T], BF16).ap()
    sc_h = dt("sc_h", [D, T], F32).ap()
    sc_p = dt("sc_p", [D, T], F32).ap()
    sc_at = dt("sc_at", [T, D], F32).ap()

    from contextlib import ExitStack
    ctx = ExitStack()
    sbuf = lambda n, s, d: ctx.enter_context(nc.sbuf_tensor(n, s, d))
    psum = lambda n, s, d: ctx.enter_context(nc.psum_tensor(n, s, d))
    mksem = lambda n: ctx.enter_context(nc.semaphore(n))

    with ctx:
        xh = sbuf("xh", [128, NT * D], F32)
        big1 = sbuf("big1", [128, ND * T], BF16)
        big2 = sbuf("big2", [128, ND * T], BF16)
        wring = sbuf("wring", [128, 12288], BF16)
        pool3 = sbuf("pool3", [128, 5120], F32)
        hE = sbuf("hE", [128, 6144], BF16)
        stg = sbuf("stg", [128, 2048], BF16)
        stats = sbuf("stats", [128, 192], F32)
        cst_bf = sbuf("cst_bf", [128, 256], BF16)
        cst_f = sbuf("cst_f", [128, 128], F32)
        onescol_b = sbuf("onescol_b", [128, 1], BF16)
        onesrow_b = sbuf("onesrow_b", [1, 128], BF16)
        onesrow_f = sbuf("onesrow_f", [1, 128], F32)
        rowpool = sbuf("rowpool", [1, 3840], F32)
        meta_sb = sbuf("meta_sb", [1, 8], I32)
        h0col = sbuf("h0col", [1, 1], F32)
        cmk = sbuf("cmk", [1, 1], F32)
        routb_sb = sbuf("routb_sb", [1, 7], F32)
        posrow = rowpool[0:1, 2816:3840]
        rs1row = rowpool[0:1, 0:1024]
        crrow = rowpool[0:1, 0:2048]
        q2row = rowpool[0:1, 0:2048]
        ln2row_sb = rowpool[0:1, 0:2048]
        k2row = rowpool[0:1, 2048:2560]
        denrow = rowpool[0:1, 2048:2560]
        mxcell = rowpool[0:1, 2560:2561]
        mx1 = rowpool[0:1, 2561:2562]
        mx2 = rowpool[0:1, 2562:2563]
        iotarow = rowpool[0:1, 2048:2304]
        nbrow = rowpool[0:1, 2816:3840].bitcast(BF16)

        ps6 = [psum(f"ps{i}", [128, 512], F32) for i in range(6)]
        s0, s1, s2, s3, s4, s5 = ps6
        p3 = psum("p3", [128, 2048], BF16)

        semnames = (["te", "ve", "sc", "gp", "qs", "qg", "cc", "so"]
                    + [f"qw{i}" for i in range(8)] + ["qh0", "qh1"]
                    + ["sp0", "sp1", "rl0", "rl1", "at0", "at1",
                       "pp0", "pp1", "pp2", "pp3", "crq", "cpq", "ocq", "orq"])
        sems = {n: mksem("s_" + n) for n in semnames}

        te = _Eng("te", auto=False)
        ve = _Eng("ve", auto=True)
        sc = _Eng("sc", auto=True)
        gp = _Eng("gp", auto=True)
        sy = _Eng("sy", auto=False)
        qs = _Dma(sy, "qs")
        qg = _Dma(gp, "qg")
        qo = _Dma(sy, "so")
        qws = [_Dma(gp, f"qw{i}") for i in range(8)]
        qw_i = [0]
        qhs = [_Dma(sy, "qh0"), _Dma(sy, "qh1")]
        qsp = [_Dma(sy, "sp0"), _Dma(sy, "sp1")]
        qrl = [_Dma(sy, "rl0"), _Dma(sy, "rl1")]
        qat = [_Dma(sy, "at0"), _Dma(sy, "at1")]
        qpp = [_Dma(sy, f"pp{i}") for i in range(4)]
        qpp_i = [0]
        pp_done = [None, None]
        crq = _Dma(sy, "crq")
        cpq = _Dma(sy, "cpq")
        ocq = _Dma(sy, "ocq")
        orq = _Dma(sy, "orq")

        p0A, p0B = _Slot(s0[:, :]), _Slot(s1[:, :])
        p1A, p1B = _Slot(s2[:, :]), _Slot(s3[:, :])
        p2A, p2B = _Slot(s4[:, :]), _Slot(s5[:, :])
        p3A, p3B = _Slot(p3[:, 0:128]), _Slot(p3[:, 1024:1152])

        PC = ts(0, 1024)
        PV = ts(1, 1024)
        PH = ts(2, 1024)
        PP = ts(3, 1024)
        PW = ts(4, 1024)
        PSQ = slice(0, 2048)
        PBC = ts(2, 1024)
        PLN2 = slice(3072, 5120)
        PXF = slice(0, 2048)
        PAF = slice(2048, 4096)
        zeros_f = hE[:, 4096:6144].bitcast(F32)

        C_NEGM, C_RSTD, C_RS1, C_RS2 = 0, 8, 16, 24
        C_BZ, C_NBZ, C_BN, C_BNP, C_LN1 = 32, 48, 64, 80, 96
        C_WRT = 112
        C_ONE, C_INVF, C_SGN, C_CAR = 168, 169, 170, 171
        C_EPS, C_E12 = 184, 185
        C_TMP = 172


        def _emit():
            env = dict(sems)
            env["sy"] = sems["qs"]
            blk = ctx.enter_context(nc.Block())

            @blk.sync
            def _(e):
                for th in sy.thunks:
                    th(e, env)

            @blk.gpsimd
            def _(e):
                for th in gp.thunks:
                    th(e, env)

            @blk.vector
            def _(e):
                for th in ve.thunks:
                    th(e, env)

            @blk.scalar
            def _(e):
                for th in sc.thunks:
                    th(e, env)

            @blk.tensor
            def _(e):
                for th in te.thunks:
                    th(e, env)

        def _early_finish():
            for i_ in range(NT):
                qo.wait(ve, ve.n)
                qo.wait(sc, sc.n)
                qo.wait(te, te.n)
                qo.wait(gp, gp.n)
                qo.start(out_hbm[ts(i_, 128), :], xh[:, ts(i_, D)])
            qof = 16 * qo.n

            def fin(e, env):
                e.wait_ge(env["so"], qof)
            sy.raw(fin)
            _emit()
            return nc

        for i in range(NT):
            qs.start(xh[:, ts(i, D)], x_in[ts(i, 128), :])
        qs.start(posrow, pos_in[:, :])
        qs.start(meta_sb[:, :], meta_in[:, :])
        qs.start(h0col[:, :], h0_in[:, :])
        qs.start(cmk[:, :], cmask_in[:, :])
        qs.start(stats[:, C_BZ:C_BZ + 16], bz_r[:, :])
        qs.start(stats[:, C_BN:C_BN + 16], bn_r[:, :])
        qs.start(stats[:, C_LN1:C_LN1 + 16], ln1_r[:, :])
        qs.start(routb_sb[:, :], routb[:, :])
        N_LD0 = qs.n

        dynv = {}

        def regload_thunk(e, env):
            for idx, nm, mx in [(0, "m_qk", 1024), (1, "m_vr", 0),
                                (2, "m_vc", 1024), (3, "m_or", 0),
                                (4, "m_oc", 1024), (5, "m_cr", 0)]:
                r = e.alloc_register(nm)
                e.reg_load(r, meta_sb[0:1, idx:idx + 1])
                dynv[nm] = e.snap(r, min_val=0, max_val=mx)
        sy.wait(qs, N_LD0)
        sy.raw(regload_thunk)

        gp.op("memset", cst_bf[:, 0:128], 0.0)
        gp.op("affine_select", out=cst_bf[:, 0:128], in_=cst_bf[:, 0:128],
              compare_op=ALU.not_equal, fill=1.0, base=0,
              pattern=[[-1, 128]], channel_multiplier=1)
        gp.op("memset", cst_bf[:, 128:256], 1.0)
        gp.op("affine_select", out=cst_bf[:, 128:256], in_=cst_bf[:, 128:256],
              compare_op=ALU.is_ge, fill=0.0, base=0,
              pattern=[[1, 128]], channel_multiplier=-1)
        gp.op("memset", cst_f[:, :], 0.0)
        gp.op("affine_select", out=cst_f[:, :], in_=cst_f[:, :],
              compare_op=ALU.not_equal, fill=1.0, base=0,
              pattern=[[-1, 128]], channel_multiplier=1)
        gp.op("memset", onescol_b[:, :], 1.0)
        gp.op("memset", onesrow_b[:, :], 1.0)
        gp.op("memset", onesrow_f[:, :], 1.0)
        gp.op("memset", hE[:, 4096:6144], 0.0)
        gp.op("memset", stats[:, C_ONE:C_ONE + 1], 1.0)
        gp.op("memset", stats[:, C_EPS:C_EPS + 1], EPS)
        gp.op("memset", stats[:, C_E12:C_E12 + 1], 1e-12)
        gp.op("iota", iotarow[0:1, 0:128], pattern=[[0, 4], [1, 32]],
              base=0, channel_multiplier=0,
              allow_small_or_imprecise_dtypes=True)
        gp.op("iota", iotarow[0:1, 128:256], pattern=[[0, 2], [1, 2], [0, 32]],
              base=0, channel_multiplier=0,
              allow_small_or_imprecise_dtypes=True)
        GP_CONST = gp.n

        ident_b = cst_bf[:, 0:128]
        causal = cst_bf[:, 128:256]
        ident_f = cst_f[:, :]

        ve.wait(qs, N_LD0)
        ve.op("tensor_scalar_mul", stats[:, C_NBZ:C_NBZ + 16],
              stats[:, C_BZ:C_BZ + 16], -1.0)
        ve.op("tensor_scalar_add", stats[:, C_BNP:C_BNP + 16],
              stats[:, C_BN:C_BN + 16], 0.5)

        te.wait(gp, GP_CONST)
        p2A.acquire(te)
        te.op_mark("matmul", s4[:, 0:1], iotarow[0:1, 0:128], onesrow_f[0:1, 0:1],
                   start=True, stop=True)
        te.op_mark("matmul", s4[:, 1:2], iotarow[0:1, 128:256], onesrow_f[0:1, 0:1],
                   start=True, stop=True)
        TE_IOTA = te.n
        sc.wait(te, TE_IOTA)
        sc.op("activation", stats[:, C_INVF:C_INVF + 1], s4[:, 0:1], AF.Exp,
              scale=-LN1E4_32)
        ve.wait(te, TE_IOTA)
        ve.wait(sc, sc.n)
        ve.op("tensor_scalar", stats[:, C_SGN:C_SGN + 1], s4[:, 1:2],
              2.0, -1.0, ALU.mult, ALU.add)
        p2A.consumed(ve)
        p2A.consumed(sc)

        for i in range(NT):
            xt = xh[:, ts(i, D)]
            ve.op("tensor_reduce", stats[:, C_TMP:C_TMP + 1], xt, AX.X, ALU.add)
            ve.op("tensor_tensor", pool3[:, PSQ], xt, xt, ALU.mult)
            ve.op("tensor_reduce", stats[:, C_TMP + 1:C_TMP + 2], pool3[:, PSQ],
                  AX.X, ALU.add)
            ve.op("tensor_scalar_mul", stats[:, C_NEGM + i:C_NEGM + i + 1],
                  stats[:, C_TMP:C_TMP + 1], -1.0 / D)
            ve.op("tensor_scalar_mul", stats[:, C_TMP + 1:C_TMP + 2],
                  stats[:, C_TMP + 1:C_TMP + 2], 1.0 / D)
            ve.op("tensor_tensor", stats[:, C_TMP + 2:C_TMP + 3],
                  stats[:, C_NEGM + i:C_NEGM + i + 1],
                  stats[:, C_NEGM + i:C_NEGM + i + 1], ALU.mult)
            ve.op("tensor_tensor", stats[:, C_TMP + 1:C_TMP + 2],
                  stats[:, C_TMP + 1:C_TMP + 2],
                  stats[:, C_TMP + 2:C_TMP + 3], ALU.subtract)
            sc.wait(ve, ve.n)
            sc.op("activation", stats[:, C_TMP + 3:C_TMP + 4],
                  stats[:, C_TMP + 1:C_TMP + 2], AF.Ln, bias=stats[:, C_EPS:C_EPS + 1])
            sc.op("activation", stats[:, C_RSTD + i:C_RSTD + i + 1],
                  stats[:, C_TMP + 3:C_TMP + 4], AF.Exp, scale=-0.5)
            ve.wait(sc, sc.n)
            ve.op("tensor_tensor", stats[:, C_TMP + 4:C_TMP + 5],
                  stats[:, C_TMP + 1:C_TMP + 2],
                  stats[:, C_RSTD + i:C_RSTD + i + 1], ALU.mult)
            ve.op("tensor_tensor", stats[:, C_TMP + 4:C_TMP + 5],
                  stats[:, C_TMP + 4:C_TMP + 5],
                  stats[:, C_RSTD + i:C_RSTD + i + 1], ALU.mult)
            sc.wait(ve, ve.n)
            sc.op("activation", stats[:, C_TMP + 5:C_TMP + 6],
                  stats[:, C_TMP + 4:C_TMP + 5], AF.Ln, bias=stats[:, C_EPS:C_EPS + 1])
            sc.op("activation", stats[:, C_RS1 + i:C_RS1 + i + 1],
                  stats[:, C_TMP + 5:C_TMP + 6], AF.Exp, scale=-0.5)
            ve.wait(sc, sc.n)
            ve.op("tensor_scalar", xt, xt,
                  stats[:, C_NEGM + i:C_NEGM + i + 1],
                  stats[:, C_RSTD + i:C_RSTD + i + 1], ALU.add, ALU.mult)
            ve.op("tensor_copy", big1[:, ts(i, D)], xt)
        VE_LN = ve.n
        SC_LN = sc.n

        te.wait(ve, VE_LN)
        p2A.acquire(te)
        p2B.acquire(te)
        for i in range(NT):
            tgt = s4 if i < 4 else s5
            te.op_raw("matmul", tgt[0:1, ts(i % 4, 128)],
                      stats[:, C_RS1 + i:C_RS1 + i + 1], ident_f,
                      start=True, stop=True)
        te.mark()
        TE_RS1ROW = te.n
        ve.wait(te, TE_RS1ROW)
        ve.op("tensor_copy", rs1row[0:1, 0:512], s4[0:1, :])
        ve.op("tensor_copy", rs1row[0:1, 512:1024], s5[0:1, :])
        p2A.consumed(ve)
        p2B.consumed(ve)
        VE_RS1ROW = ve.n
        te.wait(ve, VE_RS1ROW)
        p2A.acquire(te)
        p2B.acquire(te)
        te.op_raw("matmul", s4[:, :], onesrow_f[0:1, :], rs1row[0:1, 0:512],
                  start=True, stop=True)
        te.op_mark("matmul", s5[:, :], onesrow_f[0:1, :],
                   rs1row[0:1, 512:1024], start=True, stop=True)
        TE_RS1B = te.n
        ve.wait(te, TE_RS1B)
        ve.op("tensor_copy", pool3[:, PBC][:, 0:512], s4[:, :])
        ve.op("tensor_copy", pool3[:, PBC][:, 512:1024], s5[:, :])
        p2A.consumed(ve)
        p2B.consumed(ve)
        VE_RS1B = ve.n

        tslots = [p3A, p3B]
        tp = 0
        te.wait(ve, VE_LN)
        for ec in range(ND):
            for i in range(NT):
                sl = tslots[tp % 2]
                sl.acquire(te)
                te.op_mark("transpose", sl.view,
                           big1[:, slice(i * D + ec * 128, i * D + (ec + 1) * 128)],
                           ident_b)
                ve.wait(te, te.n)
                ve.op("tensor_copy",
                      big2[:, slice(ec * T + i * 128, ec * T + (i + 1) * 128)],
                      sl.view)
                sl.consumed(ve)
                tp += 1
        VE_XNFM = ve.n

        for ec in range(ND):
            ve.op("scalar_tensor_tensor", big1[:, ts(ec, T)], big2[:, ts(ec, T)],
                  stats[:, C_LN1 + ec:C_LN1 + ec + 1], pool3[:, PBC],
                  ALU.mult, ALU.mult)
        VE_RMS1 = ve.n

        ring = {"off": 0, "segs": []}

        RINGW = 12288

        def ring_align(a):
            off = ring["off"]
            pad = (-off) % a
            if pad:
                ring["off"] = 0 if off + pad >= RINGW else off + pad

        def wload(dram_ap3, cols_bf):
            off = ring["off"]
            if off + cols_bf > RINGW:
                off = 0
                ring["off"] = 0
            keep = []
            rel = []
            for sgm in ring["segs"]:
                if sgm[0] < off + cols_bf and sgm[1] > off:
                    rel.append(sgm[2])
                else:
                    keep.append(sgm)
            ring["segs"] = keep
            for r in rel:
                if r[0]:
                    gp.wait(te, r[0])
            dst = wring[:, off:off + cols_bf]
            inner = dram_ap3.shape[2]
            q = qws[qw_i[0] % 8]
            qw_i[0] += 1
            gp.wait(q, q.n)
            q.start(dst.rearrange("p (a b) -> p a b", b=inner), dram_ap3)
            entry = (off, off + cols_bf, [0])
            ring["segs"].append(entry)
            ring["off"] = off + cols_bf
            return dst, (q, q.n), entry[2]

        if KPHASE <= 1:
            return _early_finish()

        wz_v, wz_n, wz_r = wload(wzt[0].rearrange("(dc p) e -> p dc e", p=128), D)
        wn_v, wn_n, wn_r = wload(wnt[0].rearrange("(dc p) e -> p dc e", p=128), D)
        pk = [p0A, p0B]
        pn = [p1A, p1B]
        for ec in range(ND):
            nxt = None
            if ec + 1 < ND:
                nxt = (wload(wzt[ec + 1].rearrange("(dc p) e -> p dc e", p=128), D),
                       wload(wnt[ec + 1].rearrange("(dc p) e -> p dc e", p=128), D))
            te.wait(ve, VE_XNFM)
            te.wait(wz_n[0], wz_n[1])
            te.wait(wn_n[0], wn_n[1])
            if ec == 0:
                sc.wait(ve, VE_RMS1)
            if ec > 0:
                sc.wait(qsp[(ec - 1) % 2], qsp[(ec - 1) % 2].n)
                ve.wait(qsp[(ec - 1) % 2], qsp[(ec - 1) % 2].n)
            bzc = stats[:, C_BZ + ec:C_BZ + ec + 1]
            nbzc = stats[:, C_NBZ + ec:C_NBZ + ec + 1]
            bnc = stats[:, C_BN + ec:C_BN + ec + 1]
            bnp = stats[:, C_BNP + ec:C_BNP + ec + 1]
            for th_ in range(2):
                tsl = ts(th_, 512)
                pk[th_].acquire(te)
                for dc in range(ND):
                    te.op_raw("matmul", pk[th_].view, wz_v[:, ts(dc, 128)],
                              big2[:, slice(dc * T + th_ * 512, dc * T + th_ * 512 + 512)],
                              start=(dc == 0), stop=(dc == ND - 1))
                te.mark()
                tek = te.n
                pn[th_].acquire(te)
                for dc in range(ND):
                    te.op_raw("matmul", pn[th_].view, wn_v[:, ts(dc, 128)],
                              big2[:, slice(dc * T + th_ * 512, dc * T + th_ * 512 + 512)],
                              start=(dc == 0), stop=(dc == ND - 1))
                te.mark()
                ten = te.n
                if KGRU < 2:
                    ve.wait(te, tek)
                    ve.wait(te, ten)
                    ve.op("tensor_copy", pool3[:, PC][:, tsl], pk[th_].view)
                    pk[th_].consumed(ve)
                    ve.op("tensor_copy", pool3[:, PV][:, tsl], pn[th_].view)
                    pn[th_].consumed(ve)
                    ve.op("tensor_copy", pool3[:, PH][:, tsl], pool3[:, PC][:, tsl])
                    ve.op("tensor_copy", pool3[:, PP][:, tsl], pool3[:, PV][:, tsl])
                    continue
                sc.wait(te, tek)
                sc.op("activation", pool3[:, PC][:, tsl], pk[th_].view, AF.Sigmoid,
                      scale=-1.0, bias=nbzc)
                sc.op("activation", pool3[:, PV][:, tsl], pk[th_].view, AF.Sigmoid,
                      scale=1.0, bias=bzc)
                pk[th_].consumed(sc)
                sc.wait(te, ten)
                sc.op("activation", pool3[:, PH][:, tsl], pn[th_].view, AF.Sigmoid,
                      scale=1.0, bias=bnc)
                pn[th_].consumed(sc)
                scz = sc.n
                ve.wait(te, ten)
                ve.wait(sc, scz)
                ve.op("tensor_scalar", pool3[:, PP][:, tsl], pn[th_].view,
                      bnp, None, ALU.add)
                ve.op("tensor_scalar", pool3[:, PW][:, tsl], pn[th_].view,
                      bnc, 0.0, ALU.add, ALU.is_ge)
                pn[th_].consumed(ve)
                ve.wait(sc, scz)
                ve.op("tensor_tensor", pool3[:, PP][:, tsl], pool3[:, PP][:, tsl],
                      pool3[:, PH][:, tsl], ALU.subtract)
                ve.op("tensor_tensor", pool3[:, PP][:, tsl], pool3[:, PP][:, tsl],
                      pool3[:, PW][:, tsl], ALU.mult)
                ve.op("tensor_tensor", pool3[:, PH][:, tsl], pool3[:, PH][:, tsl],
                      pool3[:, PP][:, tsl], ALU.add)
                ve.op("tensor_tensor", pool3[:, PV][:, tsl], pool3[:, PV][:, tsl],
                      pool3[:, PH][:, tsl], ALU.mult)
            if KGRU >= 3:
                ve.op("tensor_tensor_scan", out=pool3[:, PH], data0=pool3[:, PC],
                      data1=pool3[:, PV], initial=0.0,
                      op0=ALU.mult, op1=ALU.add)
                ve.op("tensor_tensor_scan", out=pool3[:, PP], data0=pool3[:, PC],
                      data1=zeros_f, initial=1.0, op0=ALU.mult, op1=ALU.add)
            ve.op("tensor_copy", stats[:, C_CAR:C_CAR + 1],
                  pool3[:, PH][:, 1023:1024])
            VEH = ve.n
            qb = qsp[ec % 2]
            qb.begin()
            qb.wait(ve, VEH)
            if KGRU >= 4:
                qb.start(sc_h[ts(ec, 128), :], pool3[:, PH])
                qb.start(sc_p[ts(ec, 128), :], pool3[:, PP])
            SPQ, SP_N = qb, qb.n
            if KGRU >= 5:
                te.wait(ve, VEH)
                p2A.acquire(te)
                te.op_mark("matmul", s4[0:1, 0:128], stats[:, C_CAR:C_CAR + 1],
                           ident_f, start=True, stop=True)
                ve.wait(te, te.n)
                ve.op("tensor_copy", crrow[0:1, ts(ec, 128)], s4[0:1, 0:128])
                p2A.consumed(ve)
            ve.wait(SPQ, SP_N)
            wz_r[0] = te.n
            wn_r[0] = te.n
            if nxt is not None:
                (wz_v, wz_n, wz_r), (wn_v, wn_n, wn_r) = nxt
        VE_GRU = ve.n
        qo.wait(ve, VE_GRU)
        qo.start(cc_cr_in[:, :], crrow[:, :])
        N_CR = qo.n

        if KPHASE <= 2:
            return _early_finish()

        KVD_OFF = 0
        QD_OFF = NL * T
        te.wait(ve, VE_RMS1)
        dslots = [p0A, p0B, p1A, p1B]
        dsl_i = 0

        def proj_to_big2(wt, dst_off):
            nonlocal dsl_i
            loads = [wload(wt[lc].rearrange("(dc p) e -> p dc e", p=128),
                           ND * 128) for lc in range(2)]
            for lc in range(NL):
                if lc + 2 < NL:
                    loads.append(wload(
                        wt[lc + 2].rearrange("(dc p) e -> p dc e", p=128),
                        ND * 128))
                w_v, w_n, w_r = loads[lc]
                te.wait(w_n[0], w_n[1])
                for th_ in range(2):
                    sl = dslots[dsl_i % 4]
                    dsl_i += 1
                    sl.acquire(te)
                    for dc in range(ND):
                        te.op_raw("matmul", sl.view,
                                  w_v[:, slice(dc * 128, (dc + 1) * 128)],
                                  big1[:, slice(dc * T + th_ * 512,
                                                dc * T + th_ * 512 + 512)],
                                  start=(dc == 0), stop=(dc == ND - 1))
                    te.mark()
                    ve.wait(te, te.n)
                    ve.op("tensor_copy",
                          big2[:, slice(dst_off + lc * T + th_ * 512,
                                        dst_off + lc * T + th_ * 512 + 512)],
                          sl.view)
                    sl.consumed(ve)
                w_r[0] = te.n

        proj_to_big2(kvd_t, KVD_OFF)
        proj_to_big2(qd_t, QD_OFF)
        VE_KVQD = ve.n

        ku_v, ku_n, ku_r = wload(ku_w.rearrange("(lc p) e -> p lc e", p=128), 4096)
        qu_v, qu_n, qu_r = wload(qu_w.rearrange("(lc p) e -> p lc e", p=128), 4096)

        def proj_updn(wv, wn_, src_off, row_of, scale_evac):
            nonlocal dsl_i
            te.wait(ve, VE_KVQD)
            te.wait(wn_[0], wn_[1])
            for oc in range(8):
                for th_ in range(2):
                    tsl = ts(th_, 512)
                    sl = dslots[dsl_i % 4]
                    dsl_i += 1
                    sl.acquire(te)
                    for lc in range(NL):
                        te.op_raw("matmul", sl.view,
                                  wv[:, slice((lc * 8 + oc) * 128,
                                              (lc * 8 + oc + 1) * 128)],
                                  big2[:, slice(src_off + lc * T + th_ * 512,
                                                src_off + lc * T + th_ * 512 + 512)],
                                  start=(lc == 0), stop=(lc == NL - 1))
                    te.mark()
                    ve.wait(te, te.n)
                    if pp_done[th_] is not None:
                        ve.wait(*pp_done[th_])
                    if scale_evac is None:
                        ve.op("tensor_copy", stg[:, tsl], sl.view)
                    else:
                        ve.op("tensor_scalar_mul", stg[:, tsl], sl.view, scale_evac)
                    sl.consumed(ve)
                    qb = qpp[qpp_i[0] % 4]
                    qpp_i[0] += 1
                    qb.begin()
                    qb.wait(ve, ve.n)
                    for hh in range(2):
                        h = 2 * oc + hh
                        qb.start(cc_qk_in[slice(row_of(h), row_of(h) + 64),
                                          slice(th_ * 512, th_ * 512 + 512)],
                                 stg[slice(hh * 64, hh * 64 + 64), tsl])
                    pp_done[th_] = (qb, qb.n)

        proj_updn(qu_v, qu_n, QD_OFF, lambda h: h * 128, ISQ)
        qu_r[0] = te.n
        proj_updn(ku_v, ku_n, KVD_OFF, lambda h: 2048 + h * 128, None)
        ku_r[0] = te.n

        if KPHASE <= 3:
            return _early_finish()

        te.wait(qs, N_LD0)
        p2A.acquire(te)
        p2B.acquire(te)
        te.op_raw("matmul", s4[:, :], onesrow_f[0:1, :], posrow[0:1, 0:512],
                  start=True, stop=True)
        te.op_mark("matmul", s5[:, :], onesrow_f[0:1, :],
                   posrow[0:1, 512:1024], start=True, stop=True)
        TE_POSB = te.n
        SIN_Kc = slice(0, 1024)
        COS_Kc = slice(1024, 2048)
        COS_Qc = slice(2048, 3072)
        SIN_Qc = slice(3072, 4096)
        ANGc = slice(4096, 5120)
        mi_view = stg[:, 0:2048].bitcast(I32)

        ve.wait(qs, qs.n)
        if pp_done[0] is not None:
            ve.wait(*pp_done[0])
        if pp_done[1] is not None:
            ve.wait(*pp_done[1])
        ve.wait(te, TE_POSB)
        ve.op("tensor_scalar_mul", pool3[:, ANGc][:, 0:512], s4[:, :],
              stats[:, C_INVF:C_INVF + 1])
        ve.op("tensor_scalar_mul", pool3[:, ANGc][:, 512:1024], s5[:, :],
              stats[:, C_INVF:C_INVF + 1])
        p2A.consumed(ve)
        p2B.consumed(ve)

        def redm(src_sl, dst_sl):
            ve.op("tensor_scalar_mul", pool3[:, dst_sl], pool3[:, src_sl],
                  1.0 / TWO_PI)
            ve.op("tensor_copy", mi_view[:, 0:1024], pool3[:, dst_sl])
            ve.op("tensor_copy", pool3[:, dst_sl], mi_view[:, 0:1024])
            ve.op("scalar_tensor_tensor", pool3[:, dst_sl], pool3[:, dst_sl],
                  -TWO_PI, pool3[:, src_sl], ALU.mult, ALU.add)
            ve.op("tensor_single_scalar", pool3[:, COS_Qc], pool3[:, dst_sl],
                  PI, ALU.is_gt)
            ve.op("scalar_tensor_tensor", pool3[:, dst_sl], pool3[:, COS_Qc],
                  -TWO_PI, pool3[:, dst_sl], ALU.mult, ALU.add)

        redm(ANGc, SIN_Kc)
        sc.wait(ve, ve.n)
        sc.op("activation", pool3[:, SIN_Kc], pool3[:, SIN_Kc], AF.Sin)
        ve.wait(sc, sc.n)
        ve.op("tensor_scalar_add", pool3[:, ANGc], pool3[:, ANGc], PI / 2)
        redm(ANGc, COS_Kc)
        sc.wait(ve, ve.n)
        sc.op("activation", pool3[:, COS_Kc], pool3[:, COS_Kc], AF.Sin)
        ve.wait(sc, sc.n)
        ve.op("tensor_scalar_mul", pool3[:, SIN_Kc], pool3[:, SIN_Kc],
              stats[:, C_SGN:C_SGN + 1])
        ve.op("tensor_scalar_mul", pool3[:, COS_Qc], pool3[:, COS_Kc], ISQ)
        ve.op("tensor_scalar_mul", pool3[:, SIN_Qc], pool3[:, SIN_Kc], ISQ)
        VE_TAB = ve.n



        def rope_proj(wt, wst, src_off, nsrc, row_of, cosT, sinT):
            nonlocal dsl_i
            te.wait(ve, VE_TAB)
            src_big = big1 if src_off is None else big2
            off = 0 if src_off is None else src_off
            loads = [wload(wt[0].rearrange("(c p) e -> p c e", p=128),
                           nsrc * 128),
                     wload(wst[0].rearrange("(c p) e -> p c e", p=128),
                           nsrc * 128)]
            lastr = []
            for oc in range(8):
                if oc + 1 < 8:
                    loads.append(wload(
                        wt[oc + 1].rearrange("(c p) e -> p c e", p=128),
                        nsrc * 128))
                    loads.append(wload(
                        wst[oc + 1].rearrange("(c p) e -> p c e", p=128),
                        nsrc * 128))
                w_v, w_n, w_r = loads[2 * oc]
                ws_v, ws_n, ws_r = loads[2 * oc + 1]
                te.wait(w_n[0], w_n[1])
                te.wait(ws_n[0], ws_n[1])
                for th_ in range(2):
                    tsl = ts(th_, 512)
                    slA = dslots[dsl_i % 4]
                    slB = dslots[(dsl_i + 1) % 4]
                    dsl_i += 2
                    slA.acquire(te)
                    for dc in range(nsrc):
                        te.op_raw("matmul", slA.view,
                                  w_v[:, slice(dc * 128, (dc + 1) * 128)],
                                  src_big[:, slice(off + dc * T + th_ * 512,
                                                   off + dc * T + th_ * 512 + 512)],
                                  start=(dc == 0), stop=(dc == nsrc - 1))
                    slB.acquire(te)
                    for dc in range(nsrc):
                        te.op_raw("matmul", slB.view,
                                  ws_v[:, slice(dc * 128, (dc + 1) * 128)],
                                  src_big[:, slice(off + dc * T + th_ * 512,
                                                   off + dc * T + th_ * 512 + 512)],
                                  start=(dc == 0), stop=(dc == nsrc - 1))
                    te.mark()
                    w_r[0] = te.n
                    ws_r[0] = te.n
                    lastr = [w_r, ws_r]
                    ve.wait(te, te.n)
                    ve.op("tensor_tensor", pool3[:, ANGc][:, 0:512], slA.view,
                          pool3[:, cosT][:, tsl], ALU.mult)
                    ve.op("tensor_tensor", pool3[:, ANGc][:, 512:1024], slB.view,
                          pool3[:, sinT][:, tsl], ALU.mult)
                    slA.consumed(ve)
                    slB.consumed(ve)
                    if pp_done[th_] is not None:
                        ve.wait(*pp_done[th_])
                    ve.op("tensor_tensor", stg[:, tsl],
                          pool3[:, ANGc][:, 0:512], pool3[:, ANGc][:, 512:1024],
                          ALU.add)
                    qb = qpp[qpp_i[0] % 4]
                    qpp_i[0] += 1
                    qb.begin()
                    qb.wait(ve, ve.n)
                    for hh in range(2):
                        h = 2 * oc + hh
                        qb.start(cc_qk_in[slice(row_of(h) + 64, row_of(h) + 128),
                                          slice(th_ * 512, th_ * 512 + 512)],
                                 stg[slice(hh * 64, hh * 64 + 64), tsl])
                    pp_done[th_] = (qb, qb.n)

        rope_proj(rpq_t, rpqs_t, QD_OFF, NL, lambda h: h * 128, COS_Qc, SIN_Qc)
        rope_proj(rpk_t, rpks_t, None, ND, lambda h: 2048 + h * 128,
                  COS_Kc, SIN_Kc)

        vu_v, vu_n, vu_r = wload(vu_w.rearrange("(lc p) e -> p lc e", p=128),
                                 4 * D)
        te.wait(vu_n[0], vu_n[1])
        for i in range(NT):
            for ds_ in range(4):
                sl = dslots[dsl_i % 4]
                dsl_i += 1
                sl.acquire(te)
                for lc in range(NL):
                    te.op_raw("matmul", sl.view,
                              big2[:, slice(lc * T + i * 128, lc * T + (i + 1) * 128)],
                              vu_v[:, slice(lc * D + ds_ * 512,
                                            lc * D + (ds_ + 1) * 512)],
                              start=(lc == 0), stop=(lc == NL - 1))
                te.mark()
                ve.wait(te, te.n)
                if pp_done[ds_ % 2] is not None:
                    ve.wait(*pp_done[ds_ % 2])
                ve.op("tensor_copy", stg[:, ts(ds_ % 2, 512)], sl.view)
                sl.consumed(ve)
                qb = qpp[qpp_i[0] % 4]
                qpp_i[0] += 1
                qb.begin()
                qb.wait(ve, ve.n)
                qb.start(cc_vpush[ts(i, 128), ts(ds_, 512)],
                         stg[:, ts(ds_ % 2, 512)])
                pp_done[ds_ % 2] = (qb, qb.n)
        vu_r[0] = te.n

        if KPHASE <= 4:
            return _early_finish()

        gp.wait(qo, N_CR)
        for qb in qpp:
            gp.wait(qb, qb.n)

        def cc_thunk(kind, ins, outs):
            def th(e, env):
                e.collective_compute(kind, ALU.bypass,
                                     replica_groups=[[0, 1], [2, 3],
                                                     [4, 5], [6, 7]],
                                     ins=ins, outs=outs).then_inc(env["cc"])
            gp.thunks.append(th)
        cc_thunk("AllGather", [cc_cr_in[:, :]], [cc_cr_out[:, :]])
        cc_thunk("AllGather", [cc_qk_in[:, :]], [cc_qk_out[:, :]])
        cc_thunk("AllGather", [cc_vpush[:, :]], [cc_vgath[:, :]])
        CC_QKV = 3

        def crread_thunk(e, env):
            e.wait_ge(env["cc"], 1)
            e.dma_start(out=crrow[0:1, :],
                        in_=cc_cr_out[bass.ds(dynv["m_cr"], 1), :]
                        ).then_inc(env["crq"], 16)
        sy.raw(crread_thunk)
        crq.n += 1
        N_CRRD = crq.n
        ve.wait(crq, N_CRRD)
        ve.op("tensor_scalar", crrow[0:1, :], crrow[0:1, :], cmk[0:1, 0:1],
              h0col[0:1, 0:1], ALU.mult, ALU.add)
        VE_CRM = ve.n

        aslots = [p2A, p2B]
        at_done = [0, 0]
        for ec in range(ND):
            qb = qrl[ec % 2]
            qb.begin()
            qb.wait(ve, ve.n)
            if ec < 2:
                qb.wait(qsp[ec % 2], qsp[ec % 2].n)
            qb.start(pool3[:, PH], sc_h[ts(ec, 128), :])
            qb.start(pool3[:, PP], sc_p[ts(ec, 128), :])
            RLQ, ldn = qb, qb.n
            te.wait(ve, VE_CRM)
            p2A.acquire(te)
            te.op_mark("matmul", s4[:, 0:1], crrow[0:1, ts(ec, 128)],
                       onesrow_f[0:1, 0:1], start=True, stop=True)
            ve.wait(te, te.n)
            ve.op("tensor_copy", stats[:, C_CAR:C_CAR + 1], s4[:, 0:1])
            p2A.consumed(ve)
            ve.wait(RLQ, ldn)
            ve.op("scalar_tensor_tensor", pool3[:, PH], pool3[:, PP],
                  stats[:, C_CAR:C_CAR + 1], pool3[:, PH], ALU.mult, ALU.add)
            vea = ve.n
            for i in range(NT):
                sl = aslots[i % 2]
                sl.acquire(te)
                te.wait(ve, vea)
                te.op_mark("transpose", sl.view[:, 0:128],
                           pool3[:, PH][:, ts(i, 128)], ident_f)
                ve.wait(te, te.n)
                ve.wait(qat[i % 2], at_done[i % 2])
                ve.op("tensor_copy", pool3[:, PW][:, ts(i % 2, 128)],
                      sl.view[:, 0:128])
                sl.consumed(ve)
                qb2 = qat[i % 2]
                qb2.wait(ve, ve.n)
                at_done[i % 2] = qb2.start(
                    sc_at[ts(i, 128), ts(ec, 128)],
                    pool3[:, PW][:, ts(i % 2, 128)])
        N_AT = qs.n

        if KPHASE <= 5:
            return _early_finish()

        EXPT0 = 12288
        OST0 = 13312

        def ccwait_qk(e, env):
            e.wait_ge(env["cc"], 2)
        sy.raw(ccwait_qk)
        for rr in range(2):
            for band in range(2):
                cpq.start_dyn(
                    lambda env, rr=rr, band=band:
                        myqk[rr * 2048 + band * 1024: rr * 2048 + band * 1024 + 1024, :],
                    lambda env, rr=rr, band=band:
                        cc_qk_out[rr * 4096 + band * 2048:, :][
                            bass.ds(dynv["m_qk"], 1024), :], tag=f"cqk{rr}{band}")

        def ccwait_qkv(e, env):
            e.wait_ge(env["cc"], CC_QKV)
        sy.raw(ccwait_qkv)
        for rr in range(2):
            cpq.start_dyn(
                lambda env, rr=rr: myv[rr * 1024: rr * 1024 + 1024, :],
                lambda env, rr=rr:
                    cc_vgath[rr * 1024:, :][bass.ds(dynv["m_vr"], 1024),
                                            bass.ds(dynv["m_vc"], 1024)],
                tag=f"cv{rr}")
        N_COMPACT = cpq.n

        def head_load(h, slot):
            base = slot * 6144
            q = qhs[slot]
            q.begin()
            sy.wait(cpq, N_COMPACT)
            q.wait(te, te.n)
            q.wait(ve, ve.n)
            q.wait(sc, sc.n)
            for rr in range(2):
                q.start(big2[:, slice(base + rr * 1024, base + rr * 1024 + 1024)],
                        myqk[slice(rr * 2048 + 1024 + h * 128,
                                   rr * 2048 + 1024 + (h + 1) * 128), :])
                q.start(big2[:, slice(base + 2048 + rr * 1024,
                                      base + 2048 + rr * 1024 + 1024)],
                        myqk[slice(rr * 2048 + h * 128,
                                   rr * 2048 + (h + 1) * 128), :])
                q.start(big2[:, slice(base + 4096 + rr * 1024,
                                      base + 4096 + rr * 1024 + 1024)].rearrange(
                            "p (tb dv) -> p tb dv", dv=128),
                        myv[rr * 1024:(rr + 1) * 1024,
                            h * 128:(h + 1) * 128].rearrange(
                            "(tb p) dv -> p tb dv", p=128))
            return (q, q.n)

        ld_head = [None] * NH_LOC
        ld_head[0] = head_load(0, 0)
        ve.wait(*pp_done[0])
        ve.wait(*pp_done[1])
        ve.wait(qat[0], at_done[0])
        ve.wait(qat[1], at_done[1])
        o_done = [None, None]
        sT = [p0A, p0B]
        avs = [p1A, p1B]
        ex_rel = [0, 0]
        for hl in range(NH_LOC):
            slot = hl % 2
            base = slot * 6144
            Ksb = big2[:, slice(base, base + 2048)]
            Qsb = big2[:, slice(base + 2048, base + 4096)]
            Vsb = big2[:, slice(base + 4096, base + 6144)]
            if hl + 1 < NH_LOC:
                ld_head[hl + 1] = head_load(hl + 1, (hl + 1) % 2)
            ve.wait(ld_head[hl][0], ld_head[hl][1])
            ve.op("tensor_tensor", stg[:, 0:2048], Ksb, Ksb, ALU.mult)
            veksq = ve.n
            te.wait(ve, veksq)
            for half in range(2):
                p2B.acquire(te)
                te.op_raw("matmul", s5[0:1, :], onescol_b[:, 0:1],
                          stg[:, slice(half * 1024, half * 1024 + 512)],
                          start=True, stop=True)
                p2A.acquire(te)
                te.op_mark("matmul", s4[0:1, :], onescol_b[:, 0:1],
                           stg[:, slice(half * 1024 + 512, half * 1024 + 1024)],
                           start=True, stop=True)
                tk = te.n
                ve.wait(te, tk)
                ve.op("tensor_copy", k2row, s5[0:1, :])
                ve.op("tensor_reduce", mxcell if half else mxcell,
                      k2row, AX.X, ALU.max) if False else None
                ve.op("tensor_reduce", rowpool[0:1, 3585:3586], k2row, AX.X,
                      ALU.max)
                ve.op("tensor_copy", k2row, s4[0:1, :])
                ve.op("tensor_reduce", rowpool[0:1, 3586:3587], k2row, AX.X,
                      ALU.max)
                if half == 0:
                    ve.op("tensor_tensor", mxcell, rowpool[0:1, 3585:3586],
                          rowpool[0:1, 3586:3587], ALU.max)
                else:
                    ve.op("tensor_tensor", rowpool[0:1, 3585:3586],
                          rowpool[0:1, 3585:3586], rowpool[0:1, 3586:3587],
                          ALU.max)
                    ve.op("tensor_tensor", mxcell, mxcell,
                          rowpool[0:1, 3585:3586], ALU.max)
                p2A.consumed(ve)
                p2B.consumed(ve)
            ve.op("tensor_tensor", stg[:, 0:2048], Qsb, Qsb, ALU.mult)
            veqsq = ve.n
            te.wait(ve, veqsq)
            for half in range(2):
                p2B.acquire(te)
                te.op_raw("matmul", s5[0:1, :], onescol_b[:, 0:1],
                          stg[:, slice(half * 1024, half * 1024 + 512)],
                          start=True, stop=True)
                p2A.acquire(te)
                te.op_mark("matmul", s4[0:1, :], onescol_b[:, 0:1],
                           stg[:, slice(half * 1024 + 512, half * 1024 + 1024)],
                           start=True, stop=True)
                tq = te.n
                ve.wait(te, tq)
                ve.op("tensor_copy", q2row[0:1, slice(half * 1024, half * 1024 + 512)],
                      s5[0:1, :])
                ve.op("tensor_copy", q2row[0:1, slice(half * 1024 + 512,
                                                      half * 1024 + 1024)],
                      s4[0:1, :])
                p2A.consumed(ve)
                p2B.consumed(ve)
            ve.op("tensor_scalar_mul", q2row, q2row, mxcell)
            veq2 = ve.n
            sc.wait(ve, veq2)
            sc.op("activation", q2row, q2row, AF.Ln, bias=stats[0:1, C_E12:C_E12 + 1])
            sc.op("activation", q2row, q2row, AF.Exp, scale=0.5)
            ve.wait(sc, sc.n)
            ve.op("tensor_scalar", nbrow, q2row, -1.0, BSHIFT,
                  ALU.mult, ALU.add)
            venb = ve.n

            for qc in range(4):
                qbase = qc * 512
                av = avs[qc % 2]
                av.acquire(te)
                nkt = 4 * (qc + 1)
                test = {}
                exq = {}

                def emit_scores(kt):
                    qlo = max(qbase, kt * 128)
                    qw = qbase + 512 - qlo
                    stv = sT[kt % 2]
                    stv.acquire(te)
                    te.wait(ve, venb)
                    te.op_raw("matmul", stv.view[:, 0:qw], Ksb[:, ts(kt, 128)],
                              Qsb[:, slice(qlo, qbase + 512)],
                              start=True, stop=False)
                    te.op_raw("matmul", stv.view[:, 0:qw], onesrow_b[0:1, 0:128],
                              nbrow[0:1, slice(qlo, qbase + 512)],
                              start=False, stop=True)
                    te.mark()
                    test[kt] = te.n

                def emit_exp(kt):
                    qlo = max(qbase, kt * 128)
                    qw = qbase + 512 - qlo
                    stv = sT[kt % 2]
                    eslot = EXPT0 + 512 * (kt % 2)
                    ex = big2[:, slice(eslot, eslot + qw)]
                    sc.wait(te, test[kt])
                    sc.wait(te, ex_rel[kt % 2])
                    sc.op("activation", ex, stv.view[:, 0:qw], AF.Exp)
                    stv.consumed(sc)
                    dq = kt * 128 - qlo
                    if 0 <= dq < qw:
                        mw = min(128, qw - dq)
                        ve.wait(sc, sc.n)
                        ve.op("tensor_tensor", ex[:, dq:dq + mw],
                              ex[:, dq:dq + mw], causal[:, 0:mw], ALU.mult)
                        exq[kt] = (ve, ve.n)
                    else:
                        exq[kt] = (sc, sc.n)

                def emit_av(kt):
                    qlo = max(qbase, kt * 128)
                    qw = qbase + 512 - qlo
                    eslot = EXPT0 + 512 * (kt % 2)
                    ex = big2[:, slice(eslot, eslot + qw)]
                    eng, n_ = exq[kt]
                    te.wait(eng, n_)
                    if kt == 0:
                        p2A.acquire(te)
                    te.op_raw("matmul", av.view[:, slice(qlo - qbase, 512)],
                              Vsb[:, ts(kt, 128)], ex,
                              start=(kt == 0), stop=(kt == nkt - 1))
                    te.op_raw("matmul", s4[0:1, slice(qlo - qbase, 512)],
                              onescol_b[:, 0:1], ex,
                              start=(kt == 0), stop=(kt == nkt - 1))
                    te.mark()
                    ex_rel[kt % 2] = te.n

                emit_scores(0)
                emit_exp(0)
                for kt in range(nkt):
                    if kt + 1 < nkt:
                        emit_scores(kt + 1)
                        emit_exp(kt + 1)
                    emit_av(kt)
                teav = te.n
                ve.wait(te, teav)
                ve.op("tensor_copy", denrow, s4[0:1, :])
                p2A.consumed(ve)
                sc.wait(ve, ve.n)
                sc.op("activation", denrow, denrow, AF.Ln)
                sc.op("activation", denrow, denrow, AF.Exp, scale=-1.0)
                te.wait(sc, sc.n)
                p2B.acquire(te)
                te.op_mark("matmul", s5[:, :], onesrow_f[0:1, :],
                           denrow, start=True, stop=True)
                tebc = te.n
                ve.wait(te, tebc)
                ve.op("tensor_copy", pool3[:, PBC][:, 0:512], s5[:, :])
                p2B.consumed(ve)
                ost = OST0 + 512 * (qc % 2)
                if o_done[qc % 2] is not None:
                    ve.wait(*o_done[qc % 2])
                ve.op("tensor_tensor", big2[:, slice(ost, ost + 512)],
                      av.view, pool3[:, PBC][:, 0:512], ALU.mult)
                av.consumed(ve)
                veo = ve.n
                qp_ = qsp[qc % 2]
                qp_.wait(ve, veo)
                o_done[qc % 2] = (qp_, qp_.start(
                    cc_o_in[slice(hl * 128, hl * 128 + 128), ts(qc, 512)],
                    big2[:, slice(ost, ost + 512)]))

        gp.wait(qsp[0], qsp[0].n)
        gp.wait(qsp[1], qsp[1].n)
        cc_thunk("AllGather", [cc_o_in[:, :]], [cc_o_out[:, :]])
        CC_O = 4

        if KPHASE <= 6:
            return _early_finish()

        def ccwait_o(e, env):
            e.wait_ge(env["cc"], CC_O)
        sy.raw(ccwait_o)
        qs.wait(te, te.n)
        qs.wait(ve, ve.n)
        qs.wait(sc, sc.n)
        for rk in range(2):
            ocq.start_dyn(
                lambda env, rk=rk: myo[rk * 1024:(rk + 1) * 1024, :],
                lambda env, rk=rk: cc_o_out[rk * 1024:, :][
                    bass.ds(dynv["m_or"], 1024),
                    bass.ds(dynv["m_oc"], 1024)], tag=f"cO{rk}")
        N_OC = ocq.n
        sy.wait(ocq, N_OC)
        for h in range(16):
            orq.start(big2[:, ts(h, 1024)],
                      myo[slice((h // 8) * 1024 + (h % 8) * 128,
                                (h // 8) * 1024 + (h % 8) * 128 + 128), :])
        N_ORD = orq.n

        te.wait(orq, N_ORD)
        dsl_i = 0
        ring_align(4096)
        owl = [wload(ow_t[0, 0:1024].rearrange("(hc p) e -> p hc e", p=128),
                     8 * 512),
               wload(ow_t[0, 1024:2048].rearrange("(hc p) e -> p hc e", p=128),
                     8 * 512)]
        for dsl in range(4):
            if dsl + 1 < 4:
                owl.append(wload(
                    ow_t[dsl + 1, 0:1024].rearrange("(hc p) e -> p hc e", p=128),
                    8 * 512))
            vA, nA, rA = owl[2 * dsl]
            vB, nB, rB = owl[2 * dsl + 1]
            te.wait(nA[0], nA[1])
            te.wait(nB[0], nB[1])
            for i in range(NT):
                sl = dslots[dsl_i % 4]
                dsl_i += 1
                sl.acquire(te)
                for hc in range(ND):
                    vv = vA if hc < 8 else vB
                    te.op_raw("matmul", sl.view,
                              big2[:, slice(hc * 1024 + i * 128,
                                            hc * 1024 + (i + 1) * 128)],
                              vv[:, ts(hc % 8, 512)],
                              start=(hc == 0), stop=(hc == ND - 1))
                te.mark()
                ve.wait(te, te.n)
                ve.op("tensor_tensor",
                      xh[:, slice(i * D + dsl * 512, i * D + dsl * 512 + 512)],
                      xh[:, slice(i * D + dsl * 512, i * D + dsl * 512 + 512)],
                      sl.view, ALU.add)
                sl.consumed(ve)
            rA[0] = rB[0] = te.n
            if dsl + 1 < 4:
                owl.append(wload(
                    ow_t[dsl + 1, 1024:2048].rearrange("(hc p) e -> p hc e", p=128),
                    8 * 512))
        VE_H = ve.n

        crq.begin()
        crq.wait(te, te.n)
        crq.wait(ve, ve.n)
        crq.wait(sc, sc.n)
        crq.start(ln2row_sb, ln2_row[:, :])
        N_LN2 = crq.n
        te.wait(crq, N_LN2)
        for j, sl in enumerate([p0A, p0B, p1A, p1B]):
            sl.acquire(te)
            te.op_mark("matmul", sl.view, onesrow_f[0:1, :],
                       ln2row_sb[0:1, ts(j, 512)], start=True, stop=True)
            ve.wait(te, te.n)
            ve.op("tensor_copy", pool3[:, PLN2][:, ts(j, 512)], sl.view)
            sl.consumed(ve)

        for i in range(NT):
            ht = xh[:, ts(i, D)]
            ve.wait(ve, VE_H)
            ve.op("tensor_tensor", pool3[:, PSQ], ht, ht, ALU.mult)
            ve.op("tensor_reduce", stats[:, C_TMP:C_TMP + 1], pool3[:, PSQ],
                  AX.X, ALU.add)
            ve.op("tensor_scalar_mul", stats[:, C_TMP:C_TMP + 1],
                  stats[:, C_TMP:C_TMP + 1], 1.0 / D)
            sc.wait(ve, ve.n)
            sc.op("activation", stats[:, C_TMP + 1:C_TMP + 2],
                  stats[:, C_TMP:C_TMP + 1], AF.Ln, bias=stats[:, C_EPS:C_EPS + 1])
            sc.op("activation", stats[:, C_RS2 + i:C_RS2 + i + 1],
                  stats[:, C_TMP + 1:C_TMP + 2], AF.Exp, scale=-0.5)
            ve.wait(sc, sc.n)
            ve.op("scalar_tensor_tensor", big1[:, ts(i, D)], ht,
                  stats[:, C_RS2 + i:C_RS2 + i + 1], pool3[:, PLN2],
                  ALU.mult, ALU.mult)
        VE_RMS2TM = ve.n

        tp = 0
        te.wait(ve, VE_RMS2TM)
        for ec in range(ND):
            for i in range(NT):
                sl = tslots[tp % 2]
                sl.acquire(te)
                te.op_mark("transpose", sl.view,
                           big1[:, slice(i * D + ec * 128, i * D + (ec + 1) * 128)],
                           ident_b)
                ve.wait(te, te.n)
                ve.op("tensor_copy",
                      big2[:, slice(ec * T + i * 128, ec * T + (i + 1) * 128)],
                      sl.view)
                sl.consumed(ve)
                tp += 1
        VE_RMS2FM = ve.n

        rw_v, rw_n, rw_r = wload(routw.rearrange("(dc p) e -> p dc e", p=128),
                                 ND * 7)
        te.wait(rw_n[0], rw_n[1])
        te.wait(ve, VE_RMS2FM)
        for i in range(NT):
            sl = [p2A, p2B][i % 2]
            sl.acquire(te)
            for dc in range(ND):
                te.op_raw("matmul", sl.view[:, 0:7],
                          big2[:, slice(dc * T + i * 128, dc * T + (i + 1) * 128)],
                          rw_v[:, slice(dc * 7, dc * 7 + 7)],
                          start=(dc == 0), stop=(dc == ND - 1))
            te.mark()
            ve.wait(te, te.n)
            ve.op("tensor_copy", stats[:, C_WRT + 7 * i:C_WRT + 7 * i + 7],
                  sl.view[:, 0:7])
            sl.consumed(ve)
        rw_r[0] = te.n
        p2A.acquire(te)
        te.op_mark("matmul", s4[:, 0:7], onesrow_f[0:1, :], routb_sb[0:1, 0:7],
                   start=True, stop=True)
        TE_RB = te.n
        ve.wait(te, TE_RB)
        ve.op("tensor_copy", pool3[:, PW][:, 0:7], s4[:, 0:7])
        p2A.consumed(ve)
        for i in range(NT):
            lg = stats[:, C_WRT + 7 * i:C_WRT + 7 * i + 7]
            W8, W16, W24 = pool3[:, PW][:, 8:15], pool3[:, PW][:, 16:23], \
                pool3[:, PW][:, 24:31]
            t0 = stats[:, C_TMP:C_TMP + 1]
            t1 = stats[:, C_TMP + 1:C_TMP + 2]
            t2 = stats[:, C_TMP + 2:C_TMP + 3]
            t3 = stats[:, C_TMP + 3:C_TMP + 4]
            t4 = stats[:, C_TMP + 4:C_TMP + 5]
            ve.op("tensor_tensor", lg, lg, pool3[:, PW][:, 0:7], ALU.add)
            ve.op("tensor_reduce", t0, lg, AX.X, ALU.max)
            ve.op("tensor_scalar", W8, lg, t0, None, ALU.is_ge)
            ve.op("scalar_tensor_tensor", W16, W8, -1e30, lg, ALU.mult, ALU.add)
            ve.op("tensor_reduce", t1, W16, AX.X, ALU.max)
            ve.op("tensor_scalar", W24, W16, t1, None, ALU.is_ge)
            sc.wait(ve, ve.n)
            sc.op("activation", t2, t0, AF.Exp, scale=-1.0)
            sc.op("activation", t3, t1, AF.Exp, scale=-1.0)
            ve.wait(sc, sc.n)
            ve.op("tensor_scalar_add", t2, t2, 1.0)
            ve.op("tensor_scalar_add", t3, t3, 1.0)
            sc.wait(ve, ve.n)
            sc.op("activation", t2, t2, AF.Ln)
            sc.op("activation", t2, t2, AF.Exp, scale=-1.0)
            sc.op("activation", t3, t3, AF.Ln)
            sc.op("activation", t3, t3, AF.Exp, scale=-1.0)
            ve.wait(sc, sc.n)
            ve.op("tensor_tensor", t4, t2, t3, ALU.add)
            sc.wait(ve, ve.n)
            sc.op("activation", t4, t4, AF.Ln)
            sc.op("activation", t4, t4, AF.Exp, scale=-1.0)
            ve.wait(sc, sc.n)
            ve.op("tensor_scalar_mul", W8, W8, t2)
            ve.op("tensor_scalar_mul", W24, W24, t3)
            ve.op("tensor_tensor", lg, W8, W24, ALU.add)
            ve.op("tensor_scalar_mul", lg, lg, t4)
        VE_ROUTE = ve.n

        if KPHASE <= 7:
            return _early_finish()

        ring_align(4096)
        silu_rel = [0, 0]
        he_rel = [0] * 8
        for e_i in range(NE):
            gl = [wload(eg_t[e_i, 0].rearrange("(c p) i -> p c i", p=128),
                        ND * 256),
                  wload(eu_t[e_i, 0].rearrange("(c p) i -> p c i", p=128),
                        ND * 256)]
            te.wait(ve, VE_RMS2FM)
            for iq in range(4):
                if iq + 1 < 4:
                    gl.append(wload(eg_t[e_i, iq + 1].rearrange(
                        "(c p) i -> p c i", p=128), ND * 256))
                g_v, g_n, g_r = gl[2 * iq]
                u_v, u_n, u_r = gl[2 * iq + 1]
                te.wait(g_n[0], g_n[1])
                te.wait(u_n[0], u_n[1])
                for j in range(2):
                    ic = iq * 2 + j
                    for th_ in range(2):
                        sl = [p0A, p0B][th_]
                        sl.acquire(te)
                        for dc in range(ND):
                            te.op_raw("matmul", sl.view,
                                      g_v[:, slice(dc * 256 + j * 128,
                                                   dc * 256 + j * 128 + 128)],
                                      big2[:, slice(dc * T + th_ * 512,
                                                    dc * T + th_ * 512 + 512)],
                                      start=(dc == 0), stop=(dc == ND - 1))
                        te.mark()
                        sc.wait(te, te.n)
                        sc.wait(ve, silu_rel[ic % 2])
                        sc.op("activation",
                              hE[:, slice((ic % 2) * 1024 + th_ * 512,
                                          (ic % 2) * 1024 + th_ * 512 + 512)],
                              sl.view, AF.Silu)
                        sl.consumed(sc)
                    for th_ in range(2):
                        sl = [p1A, p1B][th_]
                        sl.acquire(te)
                        for dc in range(ND):
                            te.op_raw("matmul", sl.view,
                                      u_v[:, slice(dc * 256 + j * 128,
                                                   dc * 256 + j * 128 + 128)],
                                      big2[:, slice(dc * T + th_ * 512,
                                                    dc * T + th_ * 512 + 512)],
                                      start=(dc == 0), stop=(dc == ND - 1))
                        te.mark()
                        ve.wait(te, te.n)
                        ve.wait(te, he_rel[ic])
                        ve.wait(sc, sc.n)
                        ve.op("scalar_tensor_tensor",
                              big1[:, slice(ic * 1024 + th_ * 512,
                                            ic * 1024 + th_ * 512 + 512)],
                              sl.view, 0.125,
                              hE[:, slice((ic % 2) * 1024 + th_ * 512,
                                          (ic % 2) * 1024 + th_ * 512 + 512)],
                              ALU.mult, ALU.mult)
                        silu_rel[ic % 2] = ve.n
                        sl.consumed(ve)
                g_r[0] = u_r[0] = te.n
                if iq + 1 < 4:
                    gl.append(wload(eu_t[e_i, iq + 1].rearrange(
                        "(c p) i -> p c i", p=128), ND * 256))
            VE_HE = ve.n
            dl = [wload(ed_t[e_i, 0].rearrange("(c p) o -> p c o", p=128),
                        NI * 512)]
            for dsl in range(4):
                if dsl + 1 < 4:
                    dl.append(wload(ed_t[e_i, dsl + 1].rearrange(
                        "(c p) o -> p c o", p=128), NI * 512))
                d_v, d_n, d_r = dl[dsl]
                te.wait(d_n[0], d_n[1])
                te.wait(ve, VE_HE)
                for tc in range(8):
                    sl = [p2A, p2B][tc % 2]
                    sl.acquire(te)
                    for ic in range(NI):
                        te.op_raw("matmul", sl.view,
                                  big1[:, slice(ic * 1024 + tc * 128,
                                                ic * 1024 + tc * 128 + 128)],
                                  d_v[:, ts(ic, 512)],
                                  start=(ic == 0), stop=(ic == NI - 1))
                    te.mark()
                    wc = (stats[:, C_ONE:C_ONE + 1] if e_i == 0 else
                          stats[:, C_WRT + 7 * tc + (e_i - 1):
                                C_WRT + 7 * tc + e_i])
                    dst = xh[:, slice(tc * D + dsl * 512,
                                      tc * D + dsl * 512 + 512)]
                    ve.wait(te, te.n)
                    ve.wait(ve, VE_ROUTE)
                    ve.op("scalar_tensor_tensor", dst, sl.view, wc, dst,
                          ALU.mult, ALU.add)
                    sl.consumed(ve)
                d_r[0] = te.n
            for ic in range(8):
                he_rel[ic] = te.n
        VE_MOE = ve.n

        xfv = big1.bitcast(F32)
        TE_P8 = te.n
        ve_read = [0, 0]
        st_done = [qsp[0].n, qsp[1].n]
        for i in range(NT):
            s = i % 2
            xsl = slice(s * 4096, s * 4096 + 2048)
            asl = slice(s * 4096 + 2048, s * 4096 + 4096)
            qb = qrl[s]
            qb.begin()
            qb.wait(te, TE_P8)
            qb.wait(ve, ve_read[s])
            qb.wait(qsp[s], st_done[s])
            qb.wait(qat[0], at_done[0])
            qb.wait(qat[1], at_done[1])
            qb.start(xfv[:, xsl], x_in[ts(i, 128), :])
            qb.start(xfv[:, asl], sc_at[ts(i, 128), :])
            ve.wait(qb, qb.n)
            ve.op("tensor_tensor", xfv[:, xsl], xfv[:, xsl],
                  xfv[:, asl], ALU.add)
            ve.op("tensor_tensor", xfv[:, xsl], xfv[:, xsl],
                  xh[:, ts(i, D)], ALU.add)
            ve_read[s] = ve.n
            qsp[s].wait(ve, ve.n)
            st_done[s] = qsp[s].start(out_hbm[ts(i, 128), :], xfv[:, xsl])

        SP_FINAL = [16 * qsp[0].n, 16 * qsp[1].n]

        def final_thunk(e, env):
            e.wait_ge(env["sp0"], SP_FINAL[0])
            e.wait_ge(env["sp1"], SP_FINAL[1])
        sy.raw(final_thunk)

        _emit()
    return nc


_NC_CACHE = {}


def make_in_maps(L0_attn_x, L0_attn_Wz, L0_attn_bz, L0_attn_Wn, L0_attn_bn,
           L0_ffn_ln1_w, L0_ffn_ln2_w, L0_ffn_kv_d_w, L0_ffn_q_d_w,
           L0_ffn_k_u_w, L0_ffn_v_u_w, L0_ffn_q_u_w, L0_ffn_rope_k_w,
           L0_ffn_rope_q_w, L0_ffn_o_w, L0_ffn_sh_gate, L0_ffn_sh_up,
           L0_ffn_sh_down, L0_ffn_r_gate, L0_ffn_r_up, L0_ffn_r_down,
           L0_ffn_router_w, L0_ffn_router_b):
    f = np.asarray
    x = np.ascontiguousarray(f(L0_attn_x), dtype=np.float32)
    Wz = f(L0_attn_Wz); Wn = f(L0_attn_Wn)
    cvt = lambda a: np.ascontiguousarray(np.asarray(a, dtype=np.float32))

    wzt = cvt(Wz.reshape(ND, 128, D).transpose(0, 2, 1))
    wnt = cvt(Wn.reshape(ND, 128, D).transpose(0, 2, 1))
    bz_r = cvt(f(L0_attn_bz).reshape(ND, 128).T)
    bn_r = cvt(f(L0_attn_bn).reshape(ND, 128).T)
    ln1_r = cvt(f(L0_ffn_ln1_w).reshape(ND, 128).T)
    ln2_row = cvt(f(L0_ffn_ln2_w).reshape(1, D))
    kvd_t = cvt(f(L0_ffn_kv_d_w).reshape(D, NL, 128).transpose(1, 0, 2))
    qd_t = cvt(f(L0_ffn_q_d_w).reshape(D, NL, 128).transpose(1, 0, 2))
    ku_w = cvt(L0_ffn_k_u_w)
    qu_w = cvt(L0_ffn_q_u_w)
    vu_w = cvt(L0_ffn_v_u_w)
    perm = np.concatenate([np.concatenate([np.arange(32, 64), np.arange(0, 32)])
                           + 64 * h for h in range(16)])
    rpk = f(L0_ffn_rope_k_w)
    rpq = f(L0_ffn_rope_q_w)
    rpk_t = cvt(rpk.reshape(D, 8, 128).transpose(1, 0, 2))
    rpks_t = cvt(rpk[:, perm].reshape(D, 8, 128).transpose(1, 0, 2))
    rpq_t = cvt(rpq.reshape(LAT, 8, 128).transpose(1, 0, 2))
    rpqs_t = cvt(rpq[:, perm].reshape(LAT, 8, 128).transpose(1, 0, 2))
    ow_t = cvt(f(L0_ffn_o_w).reshape(D, 4, 512).transpose(1, 0, 2))
    routw = cvt(L0_ffn_router_w)
    routb = cvt(f(L0_ffn_router_b).reshape(1, 7))
    eg0 = np.concatenate([f(L0_ffn_sh_gate)[None], f(L0_ffn_r_gate)], 0)
    eu0 = np.concatenate([f(L0_ffn_sh_up)[None], f(L0_ffn_r_up)], 0)
    eg_t = cvt(eg0.reshape(NE, D, 4, 256).transpose(0, 2, 1, 3))
    eu_t = cvt(eu0.reshape(NE, D, 4, 256).transpose(0, 2, 1, 3))
    ed = np.concatenate([f(L0_ffn_sh_down)[None], f(L0_ffn_r_down)], 0)
    ed_t = cvt(ed.reshape(NE, IM, 4, 512).transpose(0, 2, 1, 3))

    shared = dict(wzt=wzt, wnt=wnt, bz_r=bz_r, bn_r=bn_r, ln1_r=ln1_r,
                  ln2_row=ln2_row, kvd_t=kvd_t, qd_t=qd_t, ku_w=ku_w,
                  qu_w=qu_w, vu_w=vu_w, rpk_t=rpk_t, rpks_t=rpks_t,
                  rpq_t=rpq_t, rpqs_t=rpqs_t, ow_t=ow_t, routw=routw,
                  routb=routb, eg_t=eg_t, eu_t=eu_t, ed_t=ed_t)

    in_maps = []
    for c in range(8):
        b, half = c // 2, c % 2
        hoff = half * 8
        m = np.zeros((1, 8), np.int32)
        m[0, 0] = hoff * 128
        m[0, 1] = 0
        m[0, 2] = hoff * 128
        m[0, 3] = 0
        m[0, 4] = half * 1024
        m[0, 5] = 0
        im = dict(shared)
        im["x"] = np.ascontiguousarray(x[b, half * T:(half + 1) * T, :])
        im["pos"] = np.arange(half * T, (half + 1) * T,
                              dtype=np.float32).reshape(1, T)
        im["h0"] = np.full((1, 1), 0.0 if half else 0.5, np.float32)
        im["cmask"] = np.full((1, 1), 1.0 if half else 0.0, np.float32)
        im["meta"] = m
        m[0, 0] = hoff * 128
        m[0, 2] = hoff * 128
        in_maps.append(im)

    return in_maps


def kernel(**inputs):
    in_maps = make_in_maps(**inputs)
    nc = build_nc()
    res = run_bass_kernel_spmd(nc, in_maps, core_ids=list(range(8)))
    out = np.empty((4, 2048, D), np.float32)
    for c in range(8):
        b, half = c // 2, c % 2
        out[b, half * T:(half + 1) * T, :] = res.results[c]["out"]
    return out


if __name__ == "__main__":
    pass
